# version 16
# baseline (speedup 1.0000x reference)
import math

import numpy as np
import ml_dtypes
import concourse.bass as bass
import concourse.mybir as mybir
from concourse.bass_utils import run_bass_kernel_spmd

F32 = mybir.dt.float32
BF16 = mybir.dt.bfloat16
AF = mybir.ActivationFunctionType
ALU = mybir.AluOpType
AXX = mybir.AxisListType.X

NCORES = 8
TC = 1024
D = 2048
H, DK, DV, M = 16, 64, 128, 64
NCH = TC // 128
NDC = D // 128
STATE_N = 128 * 512 + 128 * 1024 + 1024

DMA_SEMS = {"LCN", "LX0", "LX1", "LH", "LWA", "LWB", "LST", "LLD", "LFC",
            "LWG", "LWO", "LWS", "LR0", "LR1", "SO"}
ENG_SEMS = {"TE": "tensor", "VE": "vector", "SE": "scalar", "GE": "gpsimd"}
CC_SEMS = {"CCA", "CCG"}


class Builder:
    def __init__(self, taps=(), stop_after=None):
        self.stop_after = stop_after
        self.nc = bass.Bass("TRN2", target_bir_lowering=False,
                            detect_race_conditions=False)
        self.prog = {e: [] for e in ("sync", "gpsimd", "tensor", "vector", "scalar")}
        self.cnt = {}
        self.owner = {}
        for s in DMA_SEMS | set(ENG_SEMS) | CC_SEMS:
            self.cnt[s] = 0
            self.owner[s] = ENG_SEMS.get(s)
        self.hwm = {}
        self.semh = {}
        self._te_sig = False
        self.taps_req = set(taps)
        self.taps = {}

    def sb(self, name, shape, dtype, off):
        return self.nc.alloc_sbuf_tensor_at(name, list(shape), dtype, offset=off)

    def _filter(self, eng, waits):
        ws = []
        for w in waits:
            if w is None:
                continue
            if len(w) == 3:
                s, v, _force = w
            else:
                s, v = w
                _force = False
            if v is None or v <= 0:
                continue
            key = (eng, s)
            if self.hwm.get(key, -1) >= v:
                continue
            self.hwm[key] = v
            ws.append((s, v))
        return ws

    @staticmethod
    def force(ev):
        return None if ev is None else (ev[0], ev[1], "force")

    def op(self, eng, meth, *args, waits=(), inc=None, **kw):
        ev = None
        if inc is not None:
            self.cnt[inc] += 1
            ev = (inc, self.cnt[inc])
        ws = self._filter(eng, waits)
        pad_wait = False
        if eng == "tensor" and meth in ("matmul", "transpose"):
            has64 = any(a.base_partition() != 0 for a in args[:3]
                        if hasattr(a, "base_partition"))
            if not ws and (has64 or self._te_sig):
                pad_wait = True
            self._te_sig = has64
            if has64 and "tile_position" not in kw:
                rb = args[1].base_partition() if hasattr(args[1], "base_partition") else 0
                cb = args[0].base_partition() if hasattr(args[0], "base_partition") else 0
                kw = dict(kw, tile_position=(rb, cb))
        semh = self.semh

        def f(e, ws=ws, meth=meth, args=args, kw=kw, inc=inc, pad=pad_wait):
            if pad:
                e.wait_ge(semh["TE"], 0)
            for s, v in ws:
                e.wait_ge(semh[s], v)
            r = getattr(e, meth)(*args, **kw)
            if inc is not None:
                r.then_inc(semh[inc], 1)
        self.prog[eng].append(f)
        return ev

    def dma(self, eng, out, in_, sem, waits=(), **kw):
        self.cnt[sem] += 16
        ev = (sem, self.cnt[sem])
        ws = self._filter(eng, waits)
        semh = self.semh

        def f(e, ws=ws, out=out, in_=in_, sem=sem, kw=kw):
            for s, v in ws:
                e.wait_ge(semh[s], v)
            e.dma_start(out=out, in_=in_, **kw).then_inc(semh[sem], 16)
        self.prog[eng].append(f)
        return ev

    def _finish_early(self, y_out, ap, events):
        self.dma("sync", y_out[0:128, :], ap, "SO", waits=list(events))
        semh = self.semh
        so_final = self.cnt["SO"]
        self.prog["sync"].append(lambda e: e.wait_ge(semh["SO"], so_final))
        return self._emit()

    def tap(self, name, ap, events, shape, dtype):
        if name in self.taps_req:
            t = self.nc.dram_tensor("tap_" + name, list(shape), dtype,
                                    kind="ExternalOutput")
            self.dma("sync", t.ap(), ap, "SO", waits=list(events))
            self.taps[name] = t

    def build(self):
        nc = self.nc
        x_in = nc.dram_tensor("x", [TC, D], F32, kind="ExternalInput").ap()
        w_q = nc.dram_tensor("wq", [D, H * DK], F32, kind="ExternalInput").ap()
        w_k = nc.dram_tensor("wk", [D, H * DK], F32, kind="ExternalInput").ap()
        w_s = nc.dram_tensor("wsl", [D, H * M], F32, kind="ExternalInput").ap()
        w_v = nc.dram_tensor("wv", [D, H * DV], F32, kind="ExternalInput").ap()
        w_g = nc.dram_tensor("wg", [D, H * DV], F32, kind="ExternalInput").ap()
        w_o = nc.dram_tensor("wo", [H * DV, D], F32, kind="ExternalInput").ap()
        fac_in = nc.dram_tensor("factor", [D, D], F32, kind="ExternalInput").ap()
        gnw_in = nc.dram_tensor("gnw", [128, DV], F32, kind="ExternalInput").ap()
        bias_in = nc.dram_tensor("fbias", [128, D], F32, kind="ExternalInput").ap()
        sv_in = nc.dram_tensor("svec", [128, 8], F32, kind="ExternalInput").ap()
        cm_in = nc.dram_tensor("cmask", [128, 8], F32, kind="ExternalInput").ap()
        id_in = nc.dram_tensor("ident", [128, 128], BF16, kind="ExternalInput").ap()
        mub_in = nc.dram_tensor("masku_bf", [128, 128], BF16, kind="ExternalInput").ap()
        muf_in = nc.dram_tensor("masku_f", [128, 128], F32, kind="ExternalInput").ap()
        id64_in = nc.dram_tensor("ident64", [128, 64], BF16, kind="ExternalInput").ap()
        y_out = nc.dram_tensor("y", [TC, D], F32, kind="ExternalOutput").ap()

        h_d = nc.dram_tensor("h_d", [TC, D], BF16, kind="Internal")
        x1_d = nc.dram_tensor("x1_d", [TC, D], F32, kind="Internal")
        facA_d = nc.dram_tensor("facA_d", [D, D], F32, kind="Internal")
        facS_d = nc.dram_tensor("facS_d", [D, D], F32, kind="Internal", addr_space="Shared")
        stI_d = nc.dram_tensor("stI_d", [STATE_N], F32, kind="Internal")
        stO_d = nc.dram_tensor("stO_d", [NCORES * STATE_N], F32, kind="Internal", addr_space="Shared")

        w3 = {"q": w_q.rearrange("(c p) n -> p c n", p=128),
              "k": w_k.rearrange("(c p) n -> p c n", p=128),
              "s": w_s.rearrange("(c p) n -> p c n", p=128),
              "v": w_v.rearrange("(c p) n -> p c n", p=128),
              "g": w_g.rearrange("(c p) n -> p c n", p=128),
              "o": w_o.rearrange("(c p) n -> p c n", p=128)}
        facS3 = facS_d.ap().rearrange("(c p) n -> p c n", p=128)

        base = nc.SBUF_PARTITION_SIZE_BYTES - nc.sbuf_bytes_remaining + 576
        cur = [base]

        def R(nbytes, align=64):
            cur[0] = (cur[0] + align - 1) // align * align
            off = cur[0]
            cur[0] += nbytes
            return off

        o_ident = R(256); o_id64 = R(128); o_mub = R(256); o_muf = R(512); o_gbc = R(512)
        o_cmbc = R(32); o_svbc = R(32); o_ones = R(256)
        o_onec = R(64)
        o_stats = R(2048)
        o_QF = R(16384); o_KF = R(16384); o_HF = R(32768); o_EF = R(16384)
        o_WA = R(16384); o_WB = R(16384); o_VT = R(32768)
        o_HKr = R(2048); o_HVr = R(4096); o_Ar = R(4096)
        o_HKb = R(1024); o_HVb = R(2048); o_Ab = R(2048)
        R(1024)
        o_SC = R(32768)
        o_junk = R(4096)
        assert cur[0] <= nc.SBUF_PARTITION_SIZE_BYTES, cur[0]
        assert o_KF == o_QF + 16384 and o_WA == o_EF + 16384 and o_VT == o_WB + 16384

        sb = self.sb
        ident = sb("identt", [128, 128], BF16, o_ident)
        ident64 = sb("ident64t", [128, 64], BF16, o_id64)
        mub = sb("mubt", [128, 128], BF16, o_mub)
        muf = sb("muft", [128, 128], F32, o_muf)
        gbc = sb("gbc", [128, 128], F32, o_gbc)
        cmbc = sb("cmbc", [128, 8], F32, o_cmbc)
        svbc = sb("svbc", [128, 8], F32, o_svbc)
        onesr = sb("onesr", [1, 128], BF16, o_ones)
        onec = sb("onec", [128, 1], BF16, o_onec)
        stats = sb("stats", [128, 512], F32, o_stats)
        q_fm = sb("q_fm", [128, 8, 1024], BF16, o_QF)
        k_fm = sb("k_fm", [128, 8, 1024], BF16, o_KF)
        h_fm = sb("h_fm", [128, NDC, 1024], BF16, o_HF)
        o_tm = sb("o_tm", [128, NCH, 2048], BF16, o_HF)
        e_fm = sb("e_fm", [128, 8, 1024], BF16, o_EF)
        wbufA = sb("wbufA", [128, NDC, 512], BF16, o_WA)
        wbufB = sb("wbufB", [128, NDC, 512], BF16, o_WB)
        v_tm = sb("v_tm", [128, NCH, 2048], BF16, o_VT)
        HKr = sb("HKr", [128, 8, 64], F32, o_HKr)
        HVr = sb("HVr", [128, 8, 128], F32, o_HVr)
        Ar = sb("Ar", [1, 16, 64], F32, o_Ar)
        HKb = sb("HKb", [128, 8, 64], BF16, o_HKb)
        HVb = sb("HVb", [128, 8, 128], BF16, o_HVb)
        Ab = sb("Ab", [1, 16, 64], BF16, o_Ab)
        wgp = [sb("wg0", [128, NDC, 512], BF16, o_WA),
               sb("wg1", [128, NDC, 512], BF16, o_WB),
               sb("wg2", [128, NDC, 512], BF16, o_QF),
               sb("wg3", [128, NDC, 512], BF16, o_KF)]
        wop = [sb("woo0", [128, NDC, 512], BF16, o_EF),
               sb("woo1", [128, NDC, 512], BF16, o_VT),
               sb("woo2", [128, NDC, 512], BF16, o_VT + 16384),
               sb("woo3", [128, NDC, 512], BF16, o_HKr)]
        wsp = [sb("ws0", [128, NDC, 512], BF16, o_WA),
               sb("ws1", [128, NDC, 512], BF16, o_WB),
               sb("ws2", [128, NDC, 512], BF16, o_HF),
               sb("ws3", [128, NDC, 512], BF16, o_HF + 16384)]
        h2_fm = sb("h2_fm", [128, NDC, 1024], BF16, o_QF)
        xt = [sb("xt0", [128, D], F32, o_SC),
              sb("xt1", [128, D], F32, o_SC + 8192)]
        htm = [sb("htm0", [128, D], BF16, o_SC + 16384),
               sb("htm1", [128, D], BF16, o_SC + 20480)]
        clipsc = [sb("clip0", [128, 512], F32, o_SC),
                  sb("clip1", [128, 512], F32, o_SC + 2048)]
        okb = sb("okb", [128, 16, 64], F32, o_SC)
        pex = sb("pex", [128, 16, 64], F32, o_SC + 4096)
        rcp = sb("rcp", [128, 16, 64], F32, o_SC + 8192)
        rowst = sb("rowst", [128, 64], F32, o_SC + 12288)
        wtm_s = sb("wtm_s", [128, 2, 64], BF16, o_SC + 12544)
        wfm_s = sb("wfm_s", [128, 2, 128], BF16, o_SC + 12800)
        mq_s = sb("mq_s", [128, 2, 128], BF16, o_SC + 13312)
        mpet_s = sb("mpet_s", [128, 2, 128], BF16, o_SC + 13824)
        ktm = sb("ktm", [128, 16, 64], BF16, o_SC + 14336)
        etm = sb("etm", [128, 16, 64], BF16, o_SC + 16384)
        lndHK = sb("lndHK", [128, 512], F32, o_SC + 18432)
        lndHV = sb("lndHV", [128, 1024], F32, o_SC + 20480)
        lndA = sb("lndA", [1, 1024], F32, o_SC + 24576)
        htile = sb("htile", [128, D], BF16, o_SC)
        htfm = sb("htfm", [128, NDC, 128], BF16, o_SC + 4096)
        sgbuf = sb("sgbuf", [128, 512], BF16, o_SC + 8192)
        ofmt = sb("ofmt", [128, NDC, 128], BF16, o_SC)
        xt2 = sb("xt2", [128, D], F32, o_SC + 4096)
        attb = sb("attb", [128, D], F32, o_SC + 12288)
        x1t = sb("x1t", [128, D], F32, o_SC + 20480)
        h2tm = sb("h2tm", [128, D], BF16, o_SC + 28672)
        junkt = sb("junkt", [128, D], BF16, o_junk)
        biasbc = sb("biasbc", [128, D], F32, o_SC)
        x1r = sb("x1r", [128, D], F32, o_SC + 8192)
        mixbs = [sb("mixb", [128, D], F32, o_SC + 16384),
                 sb("mixb2", [128, D], F32, o_SC + 24576),
                 sb("mixb3", [128, D], F32, o_EF)]

        pbA = nc.alloc_psum_tensor("pbA", [128, 512], F32)
        pbB = nc.alloc_psum_tensor("pbB", [128, 512], F32)
        pbT = nc.alloc_psum_tensor("pbT", [128, 1024], BF16)
        pbC = nc.alloc_psum_tensor("pbC", [128, 512], F32)
        pbD = nc.alloc_psum_tensor("pbD", [128, 512], F32)
        pbS = nc.alloc_psum_tensor("pbS", [128, 512], F32)
        pbK = nc.alloc_psum_tensor("pbK", [128, 8, 128], BF16)
        pbE = nc.alloc_psum_tensor("pbE", [128, 8, 128], BF16)

        op, dma = self.op, self.dma

        dma("sync", ident[:, :], id_in, "LCN")
        dma("sync", ident64[:, :], id64_in, "LCN")
        dma("sync", mub[:, :], mub_in, "LCN")
        dma("sync", muf[:, :], muf_in, "LCN")
        ev_gb = dma("sync", gbc[:, :], gnw_in, "LCN")
        ev_cb = dma("sync", cmbc[:, :], cm_in, "LCN")
        ev_svb = dma("sync", svbc[:, :], sv_in, "LCN")
        cn_all = ("LCN", self.cnt["LCN"])
        ev_fst = dma("sync", facA_d.ap()[:, :], fac_in, "LFC")

        op("vector", "memset", onesr[:, :], 1.0, inc="VE")
        ev_onec = op("vector", "memset", onec[:, :], 1.0, inc="VE")
        ev_ar = op("gpsimd", "collective_compute", "AllReduce", ALU.add,
                   replica_groups=[list(range(NCORES))],
                   ins=[facA_d.ap().opt()], outs=[facS_d.ap().opt()],
                   waits=[ev_fst], inc="CCA")
        ev_z1 = op("vector", "memset", HKr[:, :, :], 0.0, inc="VE")
        ev_z2 = op("vector", "memset", HVr[:, :, :], 0.0, inc="VE")
        ev_z3 = op("vector", "memset", Ar[:, :, :], 0.0, inc="VE")

        ev_xrd = [None, None]
        ev_hrd = [[], []]
        ev_hfm_evac = []
        tpose_evac = None
        for t in range(NCH):
            b = t % 2
            sem_x = "LX0" if b == 0 else "LX1"
            wl = [ev_xrd[b]] if ev_xrd[b] else []
            ev_x = dma("sync", xt[b][:, :], x_in[t * 128:(t + 1) * 128, :], sem_x,
                       waits=wl)
            c0 = t * 6
            ev_s1 = op("scalar", "activation", htm[b][:, :], xt[b][:, :], AF.Copy,
                       accum_out=stats[:, c0:c0 + 1],
                       waits=[ev_x] + ev_hrd[b], inc="SE")
            ev_v1 = op("vector", "tensor_scalar_mul", stats[:, c0 + 1:c0 + 2],
                       stats[:, c0:c0 + 1], -1.0 / D, waits=[ev_s1], inc="VE")
            ev_s2 = op("scalar", "activation", htm[b][:, :], xt[b][:, :], AF.Square,
                       bias=stats[:, c0 + 1:c0 + 2], accum_out=stats[:, c0 + 2:c0 + 3],
                       waits=[ev_v1], inc="SE")
            ev_v2 = op("vector", "tensor_scalar", stats[:, c0 + 3:c0 + 4],
                       stats[:, c0 + 2:c0 + 3], 1.0 / D, 1e-5, ALU.mult, ALU.add,
                       waits=[ev_s2], inc="VE")
            ev_s3 = op("scalar", "activation", stats[:, c0 + 4:c0 + 5],
                       stats[:, c0 + 3:c0 + 4], AF.Sqrt, waits=[ev_v2], inc="SE")
            ev_rcp = op("vector", "reciprocal", stats[:, c0 + 5:c0 + 6],
                        stats[:, c0 + 4:c0 + 5], waits=[ev_s3], inc="VE")
            ev_v4 = op("vector", "tensor_scalar", htm[b][:, :], xt[b][:, :],
                       stats[:, c0 + 1:c0 + 2], stats[:, c0 + 5:c0 + 6],
                       ALU.add, ALU.mult,
                       waits=[ev_s2, self.force(ev_v1), self.force(ev_rcp)],
                       inc="VE")
            ev_xrd[b] = ev_v4
            ev_sp = dma("sync", h_d.ap()[t * 128:(t + 1) * 128, :], htm[b][:, :], "LH",
                        waits=[ev_v4])
            last_te = None
            for r in range(2):
                for i in range(8):
                    dc = r * 8 + i
                    w = [ev_v4, cn_all]
                    if tpose_evac is not None:
                        w.append(tpose_evac)
                    last_te = op("tensor", "transpose",
                                 pbT[:, i * 128:(i + 1) * 128],
                                 htm[b][:, dc * 128:(dc + 1) * 128], ident[:, :],
                                 waits=w, inc="TE")
                tpose_evac = op("vector", "tensor_copy",
                                h_fm[:, r * 8:(r + 1) * 8, t * 128:(t + 1) * 128],
                                pbT[:, 0:1024].rearrange("p (a b) -> p a b", a=8),
                                waits=[last_te], inc="VE")
                ev_hfm_evac.append(tpose_evac)
            ev_hrd[b] = [last_te, ev_sp]
        h_ready = ev_hfm_evac[-1]
        lh_all = ("LH", self.cnt["LH"])
        if self.stop_after == "ph1":
            return self._finish_early(y_out, xt[0][:, :], [h_ready])
        self.tap("h_fm", h_fm[:, :, :], [h_ready], [128, NDC, 1024], BF16)

        plist = [("k", 0), ("k", 1), ("s", 0), ("s", 1),
                 ("v", 0), ("v", 1), ("v", 2), ("v", 3),
                 ("q", 0), ("q", 1)]
        wbufs = [wbufA, wbufB]
        wsems = ["LWA", "LWB"]
        ev_wfree = [None, None]
        bank_ev = {}
        ev_qe, ev_ke, ev_ee, ev_ve = [], [], [], []
        clip_rd = [None, None]
        def proj_piece(pi, ev_wl_pre=None):
            wn, pc = plist[pi]
            bi = pi % 2
            if ev_wl_pre is None:
                wl = [ev_wfree[bi]] if ev_wfree[bi] else []
                ev_wl = dma("gpsimd", wbufs[bi][:, :, :],
                            w3[wn][:, :, pc * 512:(pc + 1) * 512], wsems[bi],
                            waits=wl)
            else:
                ev_wl = ev_wl_pre
            last_mm = None
            if wn in ("q", "k", "s"):
                for g in range(4):
                    for tt in range(2):
                        pb, tag = (pbA, "A") if (g * 2 + tt) % 2 == 0 else (pbB, "B")
                        for dc in range(NDC):
                            w = [ev_wl, h_ready]
                            if dc == 0 and bank_ev.get(tag):
                                w.append(bank_ev[tag])
                            last_mm = op("tensor", "matmul", pb[:, 0:512],
                                         wbufs[bi][:, dc, g * 128:(g + 1) * 128],
                                         h_fm[:, dc, tt * 512:(tt + 1) * 512],
                                         start=(dc == 0), stop=(dc == NDC - 1),
                                         waits=w,
                                         inc="TE" if dc == NDC - 1 else None)
                        cg = pc * 4 + g
                        if wn == "q":
                            ev = op("vector", "tensor_scalar_mul",
                                    q_fm[:, cg, tt * 512:(tt + 1) * 512],
                                    pb[:, 0:512], 0.125, waits=[last_mm], inc="VE")
                            ev_qe.append(ev)
                            bank_ev[tag] = ev
                        elif wn == "k":
                            ev = op("vector", "tensor_copy",
                                    k_fm[:, cg, tt * 512:(tt + 1) * 512],
                                    pb[:, 0:512], waits=[last_mm], inc="VE")
                            ev_ke.append(ev)
                            bank_ev[tag] = ev
                        else:
                            cb = (g * 2 + tt) % 2
                            w2_ = [last_mm] + ([clip_rd[cb]] if clip_rd[cb] else [])
                            evc = op("vector", "tensor_scalar", clipsc[cb][:, :],
                                     pb[:, 0:512], 32.0, -32.0, ALU.min, ALU.max,
                                     waits=w2_, inc="VE")
                            bank_ev[tag] = evc
                            eve = op("scalar", "activation",
                                     e_fm[:, cg, tt * 512:(tt + 1) * 512],
                                     clipsc[cb][:, :], AF.Exp, waits=[evc], inc="SE")
                            ev_ee.append(eve)
                            clip_rd[cb] = eve
            else:
                for tt8 in range(NCH):
                    pb, tag = (pbC, "C") if tt8 % 2 == 0 else (pbD, "D")
                    for dc in range(NDC):
                        w = [ev_wl, h_ready]
                        if dc == 0 and bank_ev.get(tag):
                            w.append(bank_ev[tag])
                        last_mm = op("tensor", "matmul", pb[:, 0:512],
                                     h_fm[:, dc, tt8 * 128:(tt8 + 1) * 128],
                                     wbufs[bi][:, dc, 0:512],
                                     start=(dc == 0), stop=(dc == NDC - 1),
                                     waits=w, inc="TE" if dc == NDC - 1 else None)
                    ev = op("vector", "tensor_copy",
                            v_tm[:, tt8, pc * 512:(pc + 1) * 512],
                            pb[:, 0:512], waits=[last_mm], inc="VE")
                    ev_ve.append(ev)
                    bank_ev[tag] = ev
            ev_wfree[bi] = last_mm
        for pi in range(8):
            proj_piece(pi)
        q_wl = []
        for pi in (8, 9):
            wn, pc = plist[pi]
            bi = pi % 2
            q_wl.append(dma("gpsimd", wbufs[bi][:, :, :],
                            w3[wn][:, :, pc * 512:(pc + 1) * 512], wsems[bi],
                            waits=[ev_wfree[bi]]))
        k_ready = ev_ke[-1]
        e_ready = ev_ee[-1]
        v_ready = ev_ve[-1]
        self.tap("e_fm", e_fm[:, :, :], [e_ready], [128, 8, 1024], BF16)
        self.tap("v_tm", v_tm[:, :, :], [v_ready], [128, NCH, 2048], BF16)

        ev_ktm = None
        ev_etm = None
        st_add = {}
        bank_rd = {}
        st_sbufrd = None

        def chunk_transposes(c, extra_waits):
            nonlocal ev_ktm, ev_etm
            lt_k = None
            lt_e = None
            for h2 in range(8):
                w = list(extra_waits)
                if h2 == 0 and ev_ktm is not None:
                    w.append(ev_ktm)
                lt_k = op("tensor", "transpose", pbK[:, h2, :],
                          k_fm[:, h2, c * 128:(c + 1) * 128],
                          ident[:, :], waits=w,
                          inc="TE" if h2 == 7 else None)
            for h2 in range(8):
                w = []
                if h2 == 0 and ev_etm is not None:
                    w.append(ev_etm)
                lt_e = op("tensor", "transpose", pbE[:, h2, :],
                          e_fm[:, h2, c * 128:(c + 1) * 128],
                          ident[:, :], waits=w,
                          inc="TE" if h2 == 7 else None)
            w = [lt_k] + ([st_sbufrd] if st_sbufrd else [])
            ev_ktm = op("vector", "tensor_copy",
                        ktm[:, :, :].rearrange("p a b -> p (a b)"),
                        pbK[:, :, :].rearrange("p a b -> p (a b)"),
                        waits=w, inc="VE")
            ev_etm = op("vector", "tensor_copy",
                        etm[:, :, :].rearrange("p a b -> p (a b)"),
                        pbE[:, :, :].rearrange("p a b -> p (a b)"),
                        waits=[lt_e], inc="VE")

        def state_quad(c, Q, extra_waits):
            nonlocal st_sbufrd
            w = list(extra_waits) + [ev_ktm, ev_etm]
            if bank_rd.get("S"):
                w.append(bank_rd["S"])
            teS1 = None
            for i in range(4):
                h = 4 * Q + i
                bp = (i % 2) * 64
                col = (i % 2) * 128 + (i // 2) * 64
                teS1 = op("tensor", "matmul", pbS[bp:bp + 64, col:col + 64],
                          ktm[:, h, :], etm[:, h, :], start=True, stop=True,
                          waits=w if i == 0 else ())
            for i in range(2):
                h = 4 * Q + i
                bp = (i % 2) * 64
                teS1 = op("tensor", "matmul",
                          pbS[bp:bp + 64, 256 + i * 128:256 + (i + 1) * 128],
                          etm[:, h, :], v_tm[:, c, h * 128:(h + 1) * 128],
                          start=True, stop=True,
                          inc="TE" if i == 1 else None)
            wC = [ev_onec]
            if bank_rd.get("C"):
                wC.append(bank_rd["C"])
            op("tensor", "matmul", pbC[0:1, 0:256], onec[:, :],
               etm[:, 4 * Q:4 * Q + 4, :].rearrange("p a b -> p (a b)"),
               start=True, stop=True, waits=wC)
            teS2 = None
            for i in range(2, 4):
                h = 4 * Q + i
                bp = (i % 2) * 64
                teS2 = op("tensor", "matmul",
                          pbC[bp:bp + 64, 256 + (i - 2) * 128:256 + (i - 1) * 128],
                          etm[:, h, :], v_tm[:, c, h * 128:(h + 1) * 128],
                          start=True, stop=True,
                          inc="TE" if i == 3 else None)
            st_sbufrd = teS2
            a1 = op("vector", "tensor_tensor",
                    HKr[0:64, 2 * Q:2 * Q + 2, :].rearrange("p a b -> p (a b)"),
                    pbS[0:64, 0:128],
                    HKr[0:64, 2 * Q:2 * Q + 2, :].rearrange("p a b -> p (a b)"),
                    ALU.add, waits=[teS1, teS2], inc="VE")
            op("vector", "tensor_tensor",
               HKr[64:128, 2 * Q:2 * Q + 2, :].rearrange("p a b -> p (a b)"),
               pbS[64:128, 128:256],
               HKr[64:128, 2 * Q:2 * Q + 2, :].rearrange("p a b -> p (a b)"),
               ALU.add)
            op("vector", "tensor_tensor", HVr[0:64, 2 * Q, :],
               pbS[0:64, 256:384], HVr[0:64, 2 * Q, :], ALU.add)
            a4 = op("vector", "tensor_tensor", HVr[64:128, 2 * Q, :],
                    pbS[64:128, 384:512], HVr[64:128, 2 * Q, :], ALU.add, inc="VE")
            op("vector", "tensor_tensor", HVr[0:64, 2 * Q + 1, :],
               pbC[0:64, 256:384], HVr[0:64, 2 * Q + 1, :], ALU.add)
            op("vector", "tensor_tensor", HVr[64:128, 2 * Q + 1, :],
               pbC[64:128, 384:512], HVr[64:128, 2 * Q + 1, :], ALU.add)
            a7 = op("vector", "tensor_tensor", Ar[0:1, 4 * Q:4 * Q + 4, :],
                    pbC[0:1, 0:256].rearrange("p (a b) -> p a b", a=4),
                    Ar[0:1, 4 * Q:4 * Q + 4, :], ALU.add, inc="VE")
            bank_rd["S"] = a4
            bank_rd["C"] = a7
            st_add["bankS"] = a7
            return a7

        last_p1 = None
        nch3 = NCH
        nh3 = H
        if self.stop_after in ("ph3t", "ph3s1", "ph3s2", "ph3s16", "ph3c2", "ph3c4"):
            nch3 = {"ph3t": 1, "ph3s1": 1, "ph3s2": 1, "ph3s16": 1,
                    "ph3c2": 2, "ph3c4": 4}[self.stop_after]
            nh3 = {"ph3t": 0, "ph3s1": 1, "ph3s2": 2, "ph3s16": H,
                   "ph3c2": H, "ph3c4": H}[self.stop_after]
        for c in range(nch3):
            chunk_transposes(c, [k_ready, e_ready, ev_z1, ev_z2, ev_z3])
            for Q in range((nh3 + 3) // 4):
                last_p1 = state_quad(c, Q, [v_ready])
        if self.stop_after in ("ph3t", "ph3s1", "ph3s2", "ph3s16", "ph3c2", "ph3c4"):
            return self._finish_early(y_out, xt[0][:, :],
                                      [ev_ktm, ev_etm] + ([last_p1] if last_p1 else []))
        self.tap("hk_loc", HKr[:, :, :], [last_p1], [128, 8, 64], F32)
        if self.stop_after == "ph3":
            return self._finish_early(y_out, xt[0][:, :], [last_p1])

        proj_piece(8, q_wl[0])
        proj_piece(9, q_wl[1])
        q_ready = ev_qe[-1]
        self.tap("q_fm", q_fm[:, :, :], [q_ready], [128, 8, 1024], BF16)
        if self.stop_after == "ph2":
            return self._finish_early(y_out, xt[0][:, :],
                                      [ev_qe[-1], ev_ke[-1], ev_ee[-1], ev_ve[-1]])

        dma("sync", stI_d.ap()[0:65536].rearrange("(p n) -> p n", p=128),
            HKr[:, :, :].rearrange("p a b -> p (a b)"), "LST", waits=[last_p1])
        dma("sync", stI_d.ap()[65536:196608].rearrange("(p n) -> p n", p=128),
            HVr[:, :, :].rearrange("p a b -> p (a b)"), "LST")
        dma("sync", stI_d.ap()[196608:197632].rearrange("(p n) -> p n", p=1),
            Ar[:, :, :].rearrange("p a b -> p (a b)"), "LST")
        st_all = ("LST", self.cnt["LST"])
        ev_ag = op("gpsimd", "collective_compute", "AllGather", ALU.bypass,
                   replica_groups=[list(range(NCORES))],
                   ins=[stI_d.ap().opt()], outs=[stO_d.ap().opt()],
                   waits=[st_all], inc="CCG")
        ev_m1 = op("vector", "memset", HKr[:, :, :], 0.0, waits=[st_all], inc="VE")
        op("vector", "memset", HVr[:, :, :], 0.0, inc="VE")
        op("vector", "memset", Ar[:, :, :], 0.0, inc="VE")
        ev_msum = None
        for i in range(NCORES):
            base = i * STATE_N
            wl = [ev_ag] if i == 0 else [ev_msum]
            dma("sync", lndHK[:, :],
                stO_d.ap()[base:base + 65536].rearrange("(p n) -> p n", p=128),
                "LLD", waits=wl)
            dma("sync", lndHV[:, :],
                stO_d.ap()[base + 65536:base + 196608].rearrange("(p n) -> p n", p=128),
                "LLD")
            dma("sync", lndA[:, :],
                stO_d.ap()[base + 196608:base + 197632].rearrange("(p n) -> p n", p=1),
                "LLD")
            ld_all = ("LLD", self.cnt["LLD"])
            op("vector", "scalar_tensor_tensor", HKr[:, :, :],
               lndHK[:, :].rearrange("p (a b) -> p a b", a=8), cmbc[:, i:i + 1],
               HKr[:, :, :], ALU.mult, ALU.add,
               waits=[ld_all, ev_m1, ev_cb, self.force(ev_msum)], inc="VE")
            op("vector", "scalar_tensor_tensor", HVr[:, :, :],
               lndHV[:, :].rearrange("p (a b) -> p a b", a=8), cmbc[:, i:i + 1],
               HVr[:, :, :], ALU.mult, ALU.add, inc="VE")
            ev_msum = op("vector", "scalar_tensor_tensor", Ar[:, :, :],
                         lndA[:, :].rearrange("p (a b) -> p a b", a=16),
                         cmbc[0:1, i:i + 1], Ar[:, :, :], ALU.mult, ALU.add,
                         inc="VE")
            st_add["bankS"] = ev_msum
        self.tap("s_in", HKr[:, :, :], [ev_msum], [128, 8, 64], F32)
        if self.stop_after == "ph4":
            return self._finish_early(y_out, xt[0][:, :], [ev_msum])

        slot_ev = {}
        ev_oev = None
        for c in range(NCH):
            wst = [self.force(st_add.get("bankS")), self.force(ev_msum)]
            if self.taps_req:
                wst.append(("SO", self.cnt["SO"]))
            evK = op("vector", "tensor_copy", HKb[:, :, :], HKr[:, :, :],
                     waits=wst, inc="VE")
            evV = op("vector", "tensor_copy", HVb[:, :, :], HVr[:, :, :], inc="VE")
            evA = op("vector", "tensor_copy", Ab[:, :, :], Ar[:, :, :], inc="VE")
            chunk_transposes(c, [ev_msum])
            for h in range(H):
                bp = (h % 2) * 64
                sl = h % 2
                c0 = h * 4
                w = [ev_ktm, ev_etm, evK, evV, evA, q_ready]
                if slot_ev.get("rdA"):
                    w.append(slot_ev["rdA"])
                mmQ = op("tensor", "matmul", pbA[:, sl * 128:(sl + 1) * 128],
                         k_fm[bp:bp + 64, h // 2, c * 128:(c + 1) * 128],
                         q_fm[bp:bp + 64, h // 2, c * 128:(c + 1) * 128],
                         start=True, stop=True, waits=w, inc="TE")
                ev_mq = op("vector", "tensor_tensor", mq_s[:, sl, :],
                           pbA[:, sl * 128:(sl + 1) * 128], muf[:, :], ALU.mult,
                           waits=[mmQ], inc="VE")
                slot_ev["rdA"] = ev_mq
                w = [ev_mq]
                if slot_ev.get("rdB"):
                    w.append(slot_ev["rdB"])
                op("tensor", "matmul", pbB[:, sl * 128:sl * 128 + 64],
                   mq_s[:, sl, :], etm[:, h, :], start=True, stop=False, waits=w)
                op("tensor", "matmul", pbB[:, sl * 128:sl * 128 + 64],
                   q_fm[bp:bp + 64, h // 2, c * 128:(c + 1) * 128],
                   HKb[bp:bp + 64, h // 2, :], start=False, stop=True)
                op("tensor", "matmul", pbB[:, sl * 128 + 64:sl * 128 + 128],
                   mub[:, :], etm[:, h, :], start=True, stop=False)
                mmD = op("tensor", "matmul", pbB[:, sl * 128 + 64:sl * 128 + 128],
                         onesr[:, :], Ab[0:1, h, :], start=False, stop=True,
                         inc="TE")
                if c == NCH - 1 and h == H - 1:
                    te_qk_free = ("TE", self.cnt["TE"])
                ev_rc = op("vector", "reciprocal", rcp[:, h, :],
                           pbB[:, sl * 128 + 64:sl * 128 + 128], waits=[mmD],
                           inc="VE")
                ev_ok = op("vector", "tensor_tensor", okb[:, h, :],
                           pbB[:, sl * 128:sl * 128 + 64], rcp[:, h, :], ALU.mult,
                           waits=[self.force(ev_rc)], inc="VE")
                slot_ev["rdB"] = ev_ok
                ev_mx = op("vector", "reduce_max", rowst[:, c0:c0 + 1],
                           okb[:, h, :], axis=AXX, negate=True,
                           waits=[self.force(ev_ok)], inc="VE")
                ev_ex = op("scalar", "activation", pex[:, h, :], okb[:, h, :],
                           AF.Exp, bias=rowst[:, c0:c0 + 1], waits=[ev_mx],
                           inc="SE")
                ev_sm = op("vector", "reduce_sum", rowst[:, c0 + 1:c0 + 2],
                           pex[:, h, :], axis=AXX, waits=[ev_ex], inc="VE")
                ev_sr2 = op("vector", "reciprocal", rowst[:, c0 + 2:c0 + 3],
                            rowst[:, c0 + 1:c0 + 2],
                            waits=[self.force(ev_sm)], inc="VE")
                w = [slot_ev[("wtm", sl)]] if slot_ev.get(("wtm", sl)) else []
                ev_w = op("vector", "scalar_tensor_tensor", wtm_s[:, sl, :],
                          pex[:, h, :], rowst[:, c0 + 2:c0 + 3], rcp[:, h, :],
                          ALU.mult, ALU.mult,
                          waits=w + [self.force(ev_sr2), self.force(ev_rc)],
                          inc="VE")
                w = [ev_w]
                if slot_ev.get("rdT"):
                    w.append(slot_ev["rdT"])
                mmW = op("tensor", "transpose",
                         pbT[bp:bp + 64, sl * 128:(sl + 1) * 128],
                         wtm_s[:, sl, :], ident[:, :], waits=w, inc="TE")
                ev_wf = op("vector", "tensor_copy", wfm_s[bp:bp + 64, sl, :],
                           pbT[bp:bp + 64, sl * 128:(sl + 1) * 128],
                           waits=[mmW], inc="VE")
                slot_ev[("wtm", sl)] = mmW
                slot_ev["rdT"] = ev_wf
                w = [ev_wf]
                if slot_ev.get("rdC"):
                    w.append(slot_ev["rdC"])
                mmP = op("tensor", "matmul", pbC[:, sl * 128:(sl + 1) * 128],
                         e_fm[bp:bp + 64, h // 2, c * 128:(c + 1) * 128],
                         wfm_s[bp:bp + 64, sl, :], start=True, stop=True,
                         waits=w, inc="TE")
                ev_mp = op("vector", "tensor_tensor", mpet_s[:, sl, :],
                           pbC[:, sl * 128:(sl + 1) * 128], muf[:, :], ALU.mult,
                           waits=[mmP], inc="VE")
                slot_ev["rdC"] = ev_mp
                w = [ev_mp]
                if slot_ev.get("rdD"):
                    w.append(slot_ev["rdD"])
                op("tensor", "matmul", pbD[:, sl * 128:(sl + 1) * 128],
                   mpet_s[:, sl, :], v_tm[:, c, h * 128:(h + 1) * 128],
                   start=True, stop=False, waits=w)
                mmO = op("tensor", "matmul", pbD[:, sl * 128:(sl + 1) * 128],
                         wfm_s[bp:bp + 64, sl, :], HVb[bp:bp + 64, h // 2, :],
                         start=False, stop=True, inc="TE")
                ev_oev = op("vector", "tensor_copy",
                            o_tm[:, c, h * 128:(h + 1) * 128],
                            pbD[:, sl * 128:(sl + 1) * 128], waits=[mmO], inc="VE")
                slot_ev["rdD"] = ev_oev
                if h % 4 == 3:
                    a7q = state_quad(c, h // 4,
                                     [slot_ev.get("rdC")])
                    slot_ev["rdC"] = a7q
            if c == 0:
                self.tap("okb0", okb[:, :, :], [ev_oev], [128, 16, 64], F32)
                self.tap("pex0", pex[:, :, :], [ev_oev], [128, 16, 64], F32)
                self.tap("rcp0", rcp[:, :, :], [ev_oev], [128, 16, 64], F32)
                self.tap("rowst0", rowst[:, :], [ev_oev], [128, 64], F32)
                self.tap("hkb0", HKb[:, :, :], [ev_oev], [128, 8, 64], BF16)
                self.tap("hvb0", HVb[:, :, :], [ev_oev], [128, 8, 128], BF16)
                self.tap("abf0", Ab[:, :, :], [ev_oev], [1, 16, 64], BF16)
        o_done = ev_oev
        self.tap("okb", okb[:, :, :], [o_done], [128, 16, 64], F32)
        self.tap("pex", pex[:, :, :], [o_done], [128, 16, 64], F32)
        self.tap("rcp", rcp[:, :, :], [o_done], [128, 16, 64], F32)
        self.tap("hkb", HKb[:, :, :], [o_done], [128, 8, 64], BF16)
        self.tap("hvb", HVb[:, :, :], [o_done], [128, 8, 128], BF16)
        self.tap("abf", Ab[:, :, :], [o_done], [1, 16, 64], BF16)
        self.tap("rowst", rowst[:, :], [o_done], [128, 64], F32)
        self.tap("hkr", HKr[:, :, :], [o_done], [128, 8, 64], F32)
        self.tap("hvr", HVr[:, :, :], [o_done], [128, 8, 128], F32)
        tap_gate = ("SO", self.cnt["SO"]) if self.taps_req else None
        te_p2 = ("TE", self.cnt["TE"])
        ve_p2 = ("VE", self.cnt["VE"])
        se_p2 = ("SE", self.cnt["SE"])
        tpose_evac = ve_p2
        self.tap("o_tm", o_tm[:, :, :], [o_done], [128, NCH, 2048], BF16)
        if self.stop_after == "ph5":
            return self._finish_early(y_out, xt[0][:, :], [o_done])

        for p in range(4):
            wts = [ev_wfree[p % 2]] if p < 2 else [te_qk_free]
            dma("gpsimd", wgp[p][:, :, :],
                w3["g"][:, :, p * 512:(p + 1) * 512], "LWG", waits=wts)
        wg_all = ("LWG", self.cnt["LWG"])
        ev_rms = {}
        for t in range(NCH):
            ev_sq = None
            for h in range(H):
                ev_sq = op("scalar", "activation", sgbuf[:, 0:128],
                           o_tm[:, t, h * 128:(h + 1) * 128], AF.Square,
                           accum_out=stats[:, 64 + t * 16 + h:65 + t * 16 + h],
                           waits=[o_done, ve_p2, tap_gate], inc="SE")
            r0 = 64 + t * 16
            i0 = 192 + t * 16
            ev_va = op("vector", "tensor_scalar", stats[:, i0:i0 + 16],
                       stats[:, r0:r0 + 16], 1.0 / DV, 1e-5, ALU.mult, ALU.add,
                       waits=[ev_sq], inc="VE")
            ev_sr = op("scalar", "activation", stats[:, i0:i0 + 16],
                       stats[:, i0:i0 + 16], AF.Sqrt, waits=[ev_va], inc="SE")
            ev_rms[t] = op("vector", "reciprocal", stats[:, i0:i0 + 16],
                           stats[:, i0:i0 + 16], waits=[ev_sr], inc="VE")
        ev_gt = {}
        ev_ht_rd = None
        ev_sg_rd = None
        bank_ev["gA"] = bank_ev["gB"] = ve_p2
        for t in range(NCH):
            ev_hl = dma("sync", htile[:, :], h_d.ap()[t * 128:(t + 1) * 128, :],
                        "LR0", waits=[lh_all, ve_p2, se_p2, ev_ht_rd, tap_gate])
            lt = None
            for r in range(2):
                for i in range(8):
                    dc = r * 8 + i
                    lt = op("tensor", "transpose", pbT[:, i * 128:(i + 1) * 128],
                            htile[:, dc * 128:(dc + 1) * 128], ident[:, :],
                            waits=[ev_hl, tpose_evac], inc="TE")
                tpose_evac = op("vector", "tensor_copy",
                                htfm[:, r * 8:(r + 1) * 8, :],
                                pbT[:, 0:1024].rearrange("p (a b) -> p a b", a=8),
                                waits=[lt], inc="VE")
            ev_ht_rd = lt
            last_g = None
            for p in range(4):
                pb, tag = (pbA, "gA") if p % 2 == 0 else (pbB, "gB")
                for dc in range(NDC):
                    w = [wg_all, tpose_evac]
                    if dc == 0:
                        w.append(bank_ev[tag])
                    last_g = op("tensor", "matmul", pb[:, 0:512],
                                htfm[:, dc, :],
                                wgp[p][:, dc, :],
                                start=(dc == 0), stop=(dc == NDC - 1),
                                waits=w, inc="TE" if dc == NDC - 1 else None)
                ev_sg = op("scalar", "activation", sgbuf[:, 0:512], pb[:, 0:512],
                           AF.Silu, waits=[last_g, ev_sg_rd], inc="SE")
                bank_ev[tag] = ev_sg
                evn = None
                for si in range(4):
                    hh = p * 4 + si
                    evn = op("vector", "scalar_tensor_tensor",
                             o_tm[:, t, hh * 128:(hh + 1) * 128],
                             o_tm[:, t, hh * 128:(hh + 1) * 128],
                             stats[:, 192 + t * 16 + hh:193 + t * 16 + hh],
                             gbc[:, :], ALU.mult, ALU.mult,
                             waits=[self.force(ev_rms[t]), ev_gb], inc="VE")
                ev_sg_rd = op("vector", "tensor_tensor",
                              o_tm[:, t, p * 512:(p + 1) * 512],
                              o_tm[:, t, p * 512:(p + 1) * 512],
                              sgbuf[:, 0:512], ALU.mult, waits=[ev_sg, evn],
                              inc="VE")
            ev_gt[t] = ev_sg_rd
        te_pA = ("TE", self.cnt["TE"])
        se_pA = ("SE", self.cnt["SE"])
        ve_pA = ("VE", self.cnt["VE"])
        self.tap("o_gated", o_tm[:, :, :], [ev_gt[NCH - 1]], [128, NCH, 2048], BF16)
        if self.stop_after == "ph6a":
            return self._finish_early(y_out, xt[0][:, :], [ev_gt[NCH - 1]])

        for p in range(4):
            dma("gpsimd", wop[p][:, :, :],
                w3["o"][:, :, p * 512:(p + 1) * 512], "LWO",
                waits=[te_p2, ve_p2])
        wo_all = ("LWO", self.cnt["LWO"])
        ev_h2ev = []
        ev_h2_rd = None
        ev_att_rd = None
        ev_xt2_rd = None
        ev_x1t_wr_gate = []
        last_x1sp = None
        bank_ev["oC"] = bank_ev["oD"] = ve_p2
        pending_h2 = None

        def emit_h2_tposes(pend):
            nonlocal tpose_evac, ev_h2_rd
            ev_nb_p, tp = pend
            lt2 = None
            for r in range(2):
                for i in range(8):
                    dc = r * 8 + i
                    lt2 = op("tensor", "transpose", pbT[:, i * 128:(i + 1) * 128],
                             h2tm[:, dc * 128:(dc + 1) * 128], ident[:, :],
                             waits=[ev_nb_p, tpose_evac], inc="TE")
                tpose_evac = op("vector", "tensor_copy",
                                h2_fm[:, r * 8:(r + 1) * 8, tp * 128:(tp + 1) * 128],
                                pbT[:, 0:1024].rearrange("p (a b) -> p a b", a=8),
                                waits=[lt2], inc="VE")
                ev_h2ev.append(tpose_evac)
            ev_h2_rd = lt2

        for t in range(NCH):
            lt = None
            for r in range(2):
                for i in range(8):
                    dvc = r * 8 + i
                    lt = op("tensor", "transpose", pbT[:, i * 128:(i + 1) * 128],
                            o_tm[:, t, dvc * 128:(dvc + 1) * 128], ident[:, :],
                            waits=[ev_gt[t], tpose_evac], inc="TE")
                tpose_evac = op("vector", "tensor_copy",
                                ofmt[:, r * 8:(r + 1) * 8, :],
                                pbT[:, 0:1024].rearrange("p (a b) -> p a b", a=8),
                                waits=[lt, se_p2], inc="VE")
            if t == NCH - 1:
                te_otm_free = ("TE", self.cnt["TE"])
            b = t % 2
            sem_x = "LX0" if b == 0 else "LX1"
            ev_x2 = dma("sync", xt2[:, :], x_in[t * 128:(t + 1) * 128, :], sem_x,
                        waits=[te_pA, se_pA, ve_pA, ev_xt2_rd])
            last_wo = None
            evat = None
            for ep in range(4):
                pb, tag = (pbC, "oC") if ep % 2 == 0 else (pbD, "oD")
                for dvc in range(NDC):
                    w = [wo_all, tpose_evac]
                    if dvc == 0:
                        w.append(bank_ev[tag])
                    last_wo = op("tensor", "matmul", pb[:, 0:512],
                                 ofmt[:, dvc, :],
                                 wop[ep][:, dvc, :],
                                 start=(dvc == 0), stop=(dvc == NDC - 1),
                                 waits=w, inc="TE" if dvc == NDC - 1 else None)
                evat = op("vector", "tensor_copy", attb[:, ep * 512:(ep + 1) * 512],
                          pb[:, 0:512], waits=[last_wo, se_p2, ev_att_rd], inc="VE")
                bank_ev[tag] = evat
            if pending_h2 is not None:
                emit_h2_tposes(pending_h2)
                pending_h2 = None
            c0 = 320 + t * 6
            ev_s1 = op("scalar", "activation", h2tm[:, :], attb[:, :], AF.Copy,
                       accum_out=stats[:, c0:c0 + 1], waits=[evat, ev_h2_rd],
                       inc="SE")
            ev_v1 = op("vector", "tensor_scalar_mul", stats[:, c0 + 1:c0 + 2],
                       stats[:, c0:c0 + 1], -1.0 / D, waits=[ev_s1], inc="VE")
            ev_s2 = op("scalar", "activation", h2tm[:, :], attb[:, :], AF.Square,
                       bias=stats[:, c0 + 1:c0 + 2], accum_out=stats[:, c0 + 2:c0 + 3],
                       waits=[ev_v1], inc="SE")
            ev_v2 = op("vector", "tensor_scalar", stats[:, c0 + 3:c0 + 4],
                       stats[:, c0 + 2:c0 + 3], 1.0 / D, 1e-5, ALU.mult, ALU.add,
                       waits=[ev_s2], inc="VE")
            ev_s3 = op("scalar", "activation", stats[:, c0 + 4:c0 + 5],
                       stats[:, c0 + 3:c0 + 4], AF.Sqrt, waits=[ev_v2], inc="SE")
            ev_rcpa = op("vector", "reciprocal", stats[:, c0 + 5:c0 + 6],
                         stats[:, c0 + 4:c0 + 5], waits=[ev_s3], inc="VE")
            ev_n = op("vector", "tensor_scalar", x1t[:, :], attb[:, :],
                      stats[:, c0 + 1:c0 + 2], stats[:, c0 + 5:c0 + 6],
                      ALU.add, ALU.mult,
                      waits=[ev_s2, self.force(ev_v1), self.force(ev_rcpa)]
                      + ev_x1t_wr_gate, inc="VE")
            ev_att_rd = ev_n
            ev_r = op("vector", "tensor_tensor", x1t[:, :], x1t[:, :], xt2[:, :],
                      ALU.add, waits=[ev_x2, self.force(ev_n)], inc="VE")
            ev_xt2_rd = ev_r
            esp = dma("sync", x1_d.ap()[t * 128:(t + 1) * 128, :], x1t[:, :], "LR1",
                      waits=[ev_r])
            last_x1sp = esp
            c1 = 368 + t * 6
            ev_s1b = op("scalar", "activation", h2tm[:, :], x1t[:, :], AF.Copy,
                        accum_out=stats[:, c1:c1 + 1], waits=[ev_r], inc="SE")
            ev_v1b = op("vector", "tensor_scalar_mul", stats[:, c1 + 1:c1 + 2],
                        stats[:, c1:c1 + 1], -1.0 / D, waits=[ev_s1b], inc="VE")
            ev_s2b = op("scalar", "activation", h2tm[:, :], x1t[:, :], AF.Square,
                        bias=stats[:, c1 + 1:c1 + 2],
                        accum_out=stats[:, c1 + 2:c1 + 3], waits=[ev_v1b], inc="SE")
            ev_v2b = op("vector", "tensor_scalar", stats[:, c1 + 3:c1 + 4],
                        stats[:, c1 + 2:c1 + 3], 1.0 / D, 1e-5, ALU.mult, ALU.add,
                        waits=[ev_s2b], inc="VE")
            ev_s3b = op("scalar", "activation", stats[:, c1 + 4:c1 + 5],
                        stats[:, c1 + 3:c1 + 4], AF.Sqrt, waits=[ev_v2b], inc="SE")
            ev_rcpb = op("vector", "reciprocal", stats[:, c1 + 5:c1 + 6],
                         stats[:, c1 + 4:c1 + 5], waits=[ev_s3b], inc="VE")
            ev_nb = op("vector", "tensor_scalar", h2tm[:, :], x1t[:, :],
                       stats[:, c1 + 1:c1 + 2], stats[:, c1 + 5:c1 + 6],
                       ALU.add, ALU.mult,
                       waits=[ev_s2b, self.force(ev_v1b), self.force(ev_rcpb)],
                       inc="VE")
            ev_x1t_wr_gate = [esp, ev_s2b]
            pending_h2 = (ev_nb, t)
        emit_h2_tposes(pending_h2)
        h2_ready = ev_h2ev[-1]
        te_pB = ("TE", self.cnt["TE"])
        ve_pB = ("VE", self.cnt["VE"])
        se_pB = ("SE", self.cnt["SE"])
        lr1_all = ("LR1", self.cnt["LR1"])
        self.tap("h2_fm", h2_fm[:, :, :], [h2_ready], [128, NDC, 1024], BF16)
        if self.stop_after == "ph6b":
            return self._finish_early(y_out, x1t[:, :], [h2_ready, last_x1sp])

        for p in range(4):
            wts = [ev_ar, te_pA, se_pA] if p < 2 else [ev_ar, te_otm_free]
            dma("gpsimd", wsp[p][:, :, :],
                facS3[:, :, p * 512:(p + 1) * 512], "LWS", waits=wts)
        ws_all = ("LWS", self.cnt["LWS"])
        ev_bb = dma("sync", biasbc[:, :], bias_in, "LCN",
                    waits=[te_pB, ve_pB, se_pB])
        mix_rd = [None, None, None]
        bank_ev["fA"] = bank_ev["fB"] = se_pA
        first_gate = [se_pB, ve_pB, te_pB, lr1_all]
        ev_xrs = {0: dma("sync", x1r[:, :], x1_d.ap()[0:128, :], "LR0",
                         waits=[lr1_all, se_pB, ve_pB])}

        def emit_ffn_ln(p, evmx_p):
            mx = mixbs[p % 3]
            c0 = 416 + p * 6
            ev_s1 = op("scalar", "activation", junkt[:, :], mx[:, :], AF.Copy,
                       accum_out=stats[:, c0:c0 + 1], waits=[evmx_p], inc="SE")
            ev_v1 = op("vector", "tensor_scalar_mul", stats[:, c0 + 1:c0 + 2],
                       stats[:, c0:c0 + 1], -1.0 / D, waits=[ev_s1], inc="VE")
            ev_s2 = op("scalar", "activation", junkt[:, :], mx[:, :], AF.Square,
                       bias=stats[:, c0 + 1:c0 + 2], accum_out=stats[:, c0 + 2:c0 + 3],
                       waits=[ev_v1], inc="SE")
            ev_v2 = op("vector", "tensor_scalar", stats[:, c0 + 3:c0 + 4],
                       stats[:, c0 + 2:c0 + 3], 1.0 / D, 1e-5, ALU.mult, ALU.add,
                       waits=[ev_s2], inc="VE")
            ev_s3 = op("scalar", "activation", stats[:, c0 + 4:c0 + 5],
                       stats[:, c0 + 3:c0 + 4], AF.Sqrt, waits=[ev_v2], inc="SE")
            ev_rcpf = op("vector", "reciprocal", stats[:, c0 + 5:c0 + 6],
                         stats[:, c0 + 4:c0 + 5], waits=[ev_s3], inc="VE")
            ev_nf = op("vector", "tensor_scalar", mx[:, :], mx[:, :],
                       stats[:, c0 + 1:c0 + 2], stats[:, c0 + 5:c0 + 6],
                       ALU.add, ALU.mult,
                       waits=[ev_s2, self.force(ev_v1), self.force(ev_rcpf)],
                       inc="VE")
            ev_y = op("vector", "tensor_tensor", mx[:, :], mx[:, :], x1r[:, :],
                      ALU.add, waits=[ev_xrs[p], self.force(ev_nf)], inc="VE")
            mix_rd[p % 3] = dma("sync", y_out[p * 128:(p + 1) * 128, :], mx[:, :],
                                "SO", waits=[ev_y])
            if p + 1 < NCH:
                ev_xrs[p + 1] = dma("sync", x1r[:, :],
                                    x1_d.ap()[(p + 1) * 128:(p + 2) * 128, :],
                                    "LR0" if p % 2 == 0 else "LR1",
                                    waits=[ev_y])

        pend_ln = None
        for t in range(NCH):
            b = t % 3
            mx = mixbs[b]
            last_f = None
            evmx = None
            for ep in range(4):
                pb, tag = (pbA, "fA") if ep % 2 == 0 else (pbB, "fB")
                for dc in range(NDC):
                    w = [ws_all, h2_ready]
                    if dc == 0:
                        w.append(bank_ev[tag])
                    last_f = op("tensor", "matmul", pb[:, 0:512],
                                h2_fm[:, dc, t * 128:(t + 1) * 128],
                                wsp[ep][:, dc, :],
                                start=(dc == 0), stop=(dc == NDC - 1),
                                waits=w, inc="TE" if dc == NDC - 1 else None)
                evmx = op("vector", "scalar_tensor_tensor",
                          mx[:, ep * 512:(ep + 1) * 512], pb[:, 0:512],
                          svbc[:, 0:1], biasbc[:, ep * 512:(ep + 1) * 512],
                          ALU.mult, ALU.add,
                          waits=[last_f, ev_svb, ev_bb, mix_rd[b]] + first_gate,
                          inc="VE")
                first_gate = []
                bank_ev[tag] = evmx
            if pend_ln is not None:
                emit_ffn_ln(*pend_ln)
            pend_ln = (t, evmx)
        emit_ffn_ln(*pend_ln)

        semh = self.semh
        so_final = self.cnt["SO"]
        self.prog["sync"].append(lambda e: e.wait_ge(semh["SO"], so_final))
        return self._emit()

    def _emit(self):
        nc = self.nc
        from contextlib import ExitStack
        with ExitStack() as st:
            for s in self.cnt:
                self.semh[s] = st.enter_context(nc.semaphore(s))
            with nc.Block() as blk:
                for eng in ("sync", "gpsimd", "tensor", "vector", "scalar"):
                    lst = self.prog[eng]

                    def body(e, lst=lst):
                        for f in lst:
                            f(e)
                    getattr(blk, eng)(body)
        return nc


def build_nc(taps=(), stop_after=None):
    return Builder(taps=taps, stop_after=stop_after).build()


def kernel(L0_attn_hidden_states, L0_attn_Wq, L0_attn_Wk, L0_attn_Wv, L0_attn_Wg,
           L0_attn_Ws, L0_attn_Wo, L0_attn_g_norm_weight, L0_ffn_factors,
           L0_ffn_bias):
    x = np.ascontiguousarray(np.asarray(L0_attn_hidden_states, dtype=np.float32)
                             .reshape(NCORES * TC, D))
    wq = np.ascontiguousarray(np.asarray(L0_attn_Wq, dtype=np.float32))
    wk = np.ascontiguousarray(np.asarray(L0_attn_Wk, dtype=np.float32))
    ws = np.ascontiguousarray(np.asarray(L0_attn_Ws, dtype=np.float32))
    wv = np.ascontiguousarray(np.asarray(L0_attn_Wv, dtype=np.float32))
    wg = np.ascontiguousarray(np.asarray(L0_attn_Wg, dtype=np.float32))
    wo = np.ascontiguousarray(np.asarray(L0_attn_Wo, dtype=np.float32))
    gnw = np.ascontiguousarray(np.tile(
        np.asarray(L0_attn_g_norm_weight, dtype=np.float32).reshape(1, DV), (128, 1)))
    fac = np.ascontiguousarray(np.asarray(L0_ffn_factors, dtype=np.float32))
    fb = np.ascontiguousarray(np.tile(
        np.asarray(L0_ffn_bias, dtype=np.float32).reshape(1, D), (128, 1)))
    sv = np.zeros((128, 8), np.float32)
    sv[:, 0] = 1.0 / math.sqrt(8.0)
    ident = np.eye(128).astype(ml_dtypes.bfloat16)
    ident64 = np.vstack([np.eye(64), np.eye(64)]).astype(ml_dtypes.bfloat16)
    masku_f = np.triu(np.ones((128, 128), np.float32))
    masku_b = masku_f.astype(ml_dtypes.bfloat16)

    nc = build_nc()
    in_maps = []
    for j in range(NCORES):
        cm = np.zeros((128, 8), np.float32)
        cm[:, :j] = 1.0
        in_maps.append({
            "x": x[j * TC:(j + 1) * TC],
            "wq": wq, "wk": wk, "wsl": ws, "wv": wv, "wg": wg, "wo": wo,
            "factor": fac[j],
            "gnw": gnw, "fbias": fb, "svec": sv, "cmask": cm,
            "ident": ident, "ident64": ident64, "masku_bf": masku_b, "masku_f": masku_f,
        })
    res = run_bass_kernel_spmd(nc, in_maps, core_ids=list(range(NCORES)))
    y = np.concatenate([r["y"] for r in res.results], axis=0)
    return y.reshape(1, NCORES * TC, D)


# version 17 (final)
# speedup vs baseline: 1.0226x; 1.0226x over previous
import math

import numpy as np
import ml_dtypes
import concourse.bass as bass
import concourse.mybir as mybir
from concourse.bass_utils import run_bass_kernel_spmd

F32 = mybir.dt.float32
BF16 = mybir.dt.bfloat16
AF = mybir.ActivationFunctionType
ALU = mybir.AluOpType
AXX = mybir.AxisListType.X

NCORES = 8
TC = 1024
D = 2048
H, DK, DV, M = 16, 64, 128, 64
NCH = TC // 128
NDC = D // 128
STATE_N = 128 * 512 + 128 * 1024 + 1024

DMA_SEMS = {"LCN", "LX0", "LX1", "LH", "LWA", "LWB", "LST", "LLD", "LFC",
            "LWG", "LWO", "LWS", "LR0", "LR1", "SO"}
ENG_SEMS = {"TE": "tensor", "VE": "vector", "SE": "scalar", "GE": "gpsimd"}
CC_SEMS = {"CCA", "CCG"}


class Builder:
    def __init__(self, taps=(), stop_after=None):
        self.stop_after = stop_after
        self.nc = bass.Bass("TRN2", target_bir_lowering=False,
                            detect_race_conditions=False)
        self.prog = {e: [] for e in ("sync", "gpsimd", "tensor", "vector", "scalar")}
        self.cnt = {}
        self.owner = {}
        for s in DMA_SEMS | set(ENG_SEMS) | CC_SEMS:
            self.cnt[s] = 0
            self.owner[s] = ENG_SEMS.get(s)
        self.hwm = {}
        self.semh = {}
        self._te_sig = False
        self.taps_req = set(taps)
        self.taps = {}

    def sb(self, name, shape, dtype, off):
        return self.nc.alloc_sbuf_tensor_at(name, list(shape), dtype, offset=off)

    def _filter(self, eng, waits):
        ws = []
        for w in waits:
            if w is None:
                continue
            if len(w) == 3:
                s, v, _force = w
            else:
                s, v = w
                _force = False
            if v is None or v <= 0:
                continue
            key = (eng, s)
            if self.hwm.get(key, -1) >= v:
                continue
            self.hwm[key] = v
            ws.append((s, v))
        return ws

    @staticmethod
    def force(ev):
        return None if ev is None else (ev[0], ev[1], "force")

    def op(self, eng, meth, *args, waits=(), inc=None, **kw):
        ev = None
        if inc is not None:
            self.cnt[inc] += 1
            ev = (inc, self.cnt[inc])
        ws = self._filter(eng, waits)
        pad_wait = False
        if eng == "tensor" and meth in ("matmul", "transpose"):
            has64 = any(a.base_partition() != 0 for a in args[:3]
                        if hasattr(a, "base_partition"))
            if not ws and (has64 or self._te_sig):
                pad_wait = True
            self._te_sig = has64
            if has64 and "tile_position" not in kw:
                rb = args[1].base_partition() if hasattr(args[1], "base_partition") else 0
                cb = args[0].base_partition() if hasattr(args[0], "base_partition") else 0
                kw = dict(kw, tile_position=(rb, cb))
        semh = self.semh

        def f(e, ws=ws, meth=meth, args=args, kw=kw, inc=inc, pad=pad_wait):
            if pad:
                e.wait_ge(semh["TE"], 0)
            for s, v in ws:
                e.wait_ge(semh[s], v)
            r = getattr(e, meth)(*args, **kw)
            if inc is not None:
                r.then_inc(semh[inc], 1)
        self.prog[eng].append(f)
        return ev

    def dma(self, eng, out, in_, sem, waits=(), **kw):
        self.cnt[sem] += 16
        ev = (sem, self.cnt[sem])
        ws = self._filter(eng, waits)
        semh = self.semh

        def f(e, ws=ws, out=out, in_=in_, sem=sem, kw=kw):
            for s, v in ws:
                e.wait_ge(semh[s], v)
            e.dma_start(out=out, in_=in_, **kw).then_inc(semh[sem], 16)
        self.prog[eng].append(f)
        return ev

    def _finish_early(self, y_out, ap, events):
        self.dma("sync", y_out[0:128, :], ap, "SO", waits=list(events))
        semh = self.semh
        so_final = self.cnt["SO"]
        self.prog["sync"].append(lambda e: e.wait_ge(semh["SO"], so_final))
        return self._emit()

    def tap(self, name, ap, events, shape, dtype):
        if name in self.taps_req:
            t = self.nc.dram_tensor("tap_" + name, list(shape), dtype,
                                    kind="ExternalOutput")
            self.dma("sync", t.ap(), ap, "SO", waits=list(events))
            self.taps[name] = t

    def build(self):
        nc = self.nc
        x_in = nc.dram_tensor("x", [TC, D], F32, kind="ExternalInput").ap()
        w_q = nc.dram_tensor("wq", [D, H * DK], F32, kind="ExternalInput").ap()
        w_k = nc.dram_tensor("wk", [D, H * DK], F32, kind="ExternalInput").ap()
        w_s = nc.dram_tensor("wsl", [D, H * M], F32, kind="ExternalInput").ap()
        w_v = nc.dram_tensor("wv", [D, H * DV], F32, kind="ExternalInput").ap()
        w_g = nc.dram_tensor("wg", [D, H * DV], F32, kind="ExternalInput").ap()
        w_o = nc.dram_tensor("wo", [H * DV, D], F32, kind="ExternalInput").ap()
        fac_in = nc.dram_tensor("factor", [D, D], F32, kind="ExternalInput").ap()
        gnw_in = nc.dram_tensor("gnw", [128, DV], F32, kind="ExternalInput").ap()
        bias_in = nc.dram_tensor("fbias", [128, D], F32, kind="ExternalInput").ap()
        sv_in = nc.dram_tensor("svec", [128, 8], F32, kind="ExternalInput").ap()
        cm_in = nc.dram_tensor("cmask", [128, 8], F32, kind="ExternalInput").ap()
        id_in = nc.dram_tensor("ident", [128, 128], BF16, kind="ExternalInput").ap()
        mub_in = nc.dram_tensor("masku_bf", [128, 128], BF16, kind="ExternalInput").ap()
        muf_in = nc.dram_tensor("masku_f", [128, 128], F32, kind="ExternalInput").ap()
        id64_in = nc.dram_tensor("ident64", [128, 64], BF16, kind="ExternalInput").ap()
        y_out = nc.dram_tensor("y", [TC, D], F32, kind="ExternalOutput").ap()

        h_d = nc.dram_tensor("h_d", [TC, D], BF16, kind="Internal")
        x1_d = nc.dram_tensor("x1_d", [TC, D], F32, kind="Internal")
        facA_d = nc.dram_tensor("facA_d", [D, D], F32, kind="Internal")
        facS_d = nc.dram_tensor("facS_d", [D, D], F32, kind="Internal", addr_space="Shared")
        stI_d = nc.dram_tensor("stI_d", [STATE_N], F32, kind="Internal")
        stO_d = nc.dram_tensor("stO_d", [NCORES * STATE_N], F32, kind="Internal", addr_space="Shared")

        w3 = {"q": w_q.rearrange("(c p) n -> p c n", p=128),
              "k": w_k.rearrange("(c p) n -> p c n", p=128),
              "s": w_s.rearrange("(c p) n -> p c n", p=128),
              "v": w_v.rearrange("(c p) n -> p c n", p=128),
              "g": w_g.rearrange("(c p) n -> p c n", p=128),
              "o": w_o.rearrange("(c p) n -> p c n", p=128)}
        facS3 = facS_d.ap().rearrange("(c p) n -> p c n", p=128)

        base = nc.SBUF_PARTITION_SIZE_BYTES - nc.sbuf_bytes_remaining + 576
        cur = [base]

        def R(nbytes, align=64):
            cur[0] = (cur[0] + align - 1) // align * align
            off = cur[0]
            cur[0] += nbytes
            return off

        o_ident = R(256); o_id64 = R(128); o_mub = R(256); o_muf = R(512); o_gbc = R(512)
        o_cmbc = R(32); o_svbc = R(32); o_ones = R(256)
        o_onec = R(64)
        o_stats = R(2048)
        o_QF = R(16384); o_KF = R(16384); o_HF = R(32768); o_EF = R(16384)
        o_WA = R(16384); o_WB = R(16384); o_VT = R(32768)
        o_HKr = R(2048); o_HVr = R(4096); o_Ar = R(4096)
        o_HKb = R(1024); o_HVb = R(2048); o_Ab = R(2048)
        R(1024)
        o_SC = R(32768)
        o_junk = R(4096)
        assert cur[0] <= nc.SBUF_PARTITION_SIZE_BYTES, cur[0]
        assert o_KF == o_QF + 16384 and o_WA == o_EF + 16384 and o_VT == o_WB + 16384

        sb = self.sb
        ident = sb("identt", [128, 128], BF16, o_ident)
        ident64 = sb("ident64t", [128, 64], BF16, o_id64)
        mub = sb("mubt", [128, 128], BF16, o_mub)
        muf = sb("muft", [128, 128], F32, o_muf)
        gbc = sb("gbc", [128, 128], F32, o_gbc)
        cmbc = sb("cmbc", [128, 8], F32, o_cmbc)
        svbc = sb("svbc", [128, 8], F32, o_svbc)
        onesr = sb("onesr", [1, 128], BF16, o_ones)
        onec = sb("onec", [128, 1], BF16, o_onec)
        stats = sb("stats", [128, 512], F32, o_stats)
        q_fm = sb("q_fm", [128, 8, 1024], BF16, o_QF)
        k_fm = sb("k_fm", [128, 8, 1024], BF16, o_KF)
        h_fm = sb("h_fm", [128, NDC, 1024], BF16, o_HF)
        o_tm = sb("o_tm", [128, NCH, 2048], BF16, o_HF)
        e_fm = sb("e_fm", [128, 8, 1024], BF16, o_EF)
        wbufA = sb("wbufA", [128, NDC, 512], BF16, o_WA)
        wbufB = sb("wbufB", [128, NDC, 512], BF16, o_WB)
        v_tm = sb("v_tm", [128, NCH, 2048], BF16, o_VT)
        HKr = sb("HKr", [128, 8, 64], F32, o_HKr)
        HVr = sb("HVr", [128, 8, 128], F32, o_HVr)
        Ar = sb("Ar", [1, 16, 64], F32, o_Ar)
        HKb = sb("HKb", [128, 8, 64], BF16, o_HKb)
        HVb = sb("HVb", [128, 8, 128], BF16, o_HVb)
        Ab = sb("Ab", [1, 16, 64], BF16, o_Ab)
        wgp = [sb("wg0", [128, NDC, 512], BF16, o_WA),
               sb("wg1", [128, NDC, 512], BF16, o_WB),
               sb("wg2", [128, NDC, 512], BF16, o_QF),
               sb("wg3", [128, NDC, 512], BF16, o_KF)]
        wop = [sb("woo0", [128, NDC, 512], BF16, o_EF),
               sb("woo1", [128, NDC, 512], BF16, o_VT),
               sb("woo2", [128, NDC, 512], BF16, o_VT + 16384),
               sb("woo3", [128, NDC, 512], BF16, o_HKr)]
        wsp = [sb("ws0", [128, NDC, 512], BF16, o_WA),
               sb("ws1", [128, NDC, 512], BF16, o_WB),
               sb("ws2", [128, NDC, 512], BF16, o_HF),
               sb("ws3", [128, NDC, 512], BF16, o_HF + 16384)]
        h2_fm = sb("h2_fm", [128, NDC, 1024], BF16, o_QF)
        xt = [sb("xt0", [128, D], F32, o_SC),
              sb("xt1", [128, D], F32, o_SC + 8192)]
        htm = [sb("htm0", [128, D], BF16, o_SC + 16384),
               sb("htm1", [128, D], BF16, o_SC + 20480)]
        clipsc = [sb("clip0", [128, 512], F32, o_SC),
                  sb("clip1", [128, 512], F32, o_SC + 2048)]
        okb = sb("okb", [128, 16, 64], F32, o_SC)
        pex = sb("pex", [128, 16, 64], F32, o_SC + 4096)
        rcp = sb("rcp", [128, 16, 64], F32, o_SC + 8192)
        rowst = sb("rowst", [128, 64], F32, o_SC + 12288)
        wtm_s = sb("wtm_s", [128, 2, 64], BF16, o_SC + 12544)
        wfm_s = sb("wfm_s", [128, 2, 128], BF16, o_SC + 12800)
        mq_s = sb("mq_s", [128, 2, 128], BF16, o_SC + 13312)
        mpet_s = sb("mpet_s", [128, 2, 128], BF16, o_SC + 13824)
        ktm = sb("ktm", [128, 16, 64], BF16, o_SC + 14336)
        etm = sb("etm", [128, 16, 64], BF16, o_SC + 16384)
        lndHK = sb("lndHK", [128, 512], F32, o_SC + 18432)
        lndHV = sb("lndHV", [128, 1024], F32, o_SC + 20480)
        lndA = sb("lndA", [1, 1024], F32, o_SC + 24576)
        htile = sb("htile", [128, D], BF16, o_SC)
        htfm = sb("htfm", [128, NDC, 128], BF16, o_SC + 4096)
        sgbuf = sb("sgbuf", [128, 512], BF16, o_SC + 8192)
        ofmt = sb("ofmt", [128, NDC, 128], BF16, o_SC)
        xt2 = sb("xt2", [128, D], F32, o_SC + 4096)
        attb = sb("attb", [128, D], F32, o_SC + 12288)
        x1t = sb("x1t", [128, D], F32, o_SC + 20480)
        h2tm = sb("h2tm", [128, D], BF16, o_SC + 28672)
        junkt = sb("junkt", [128, D], BF16, o_junk)
        biasbc = sb("biasbc", [128, D], F32, o_SC)
        x1r = sb("x1r", [128, D], F32, o_SC + 8192)
        mixbs = [sb("mixb", [128, D], F32, o_SC + 16384),
                 sb("mixb2", [128, D], F32, o_SC + 24576),
                 sb("mixb3", [128, D], F32, o_EF)]

        pbA = nc.alloc_psum_tensor("pbA", [128, 512], F32)
        pbB = nc.alloc_psum_tensor("pbB", [128, 512], F32)
        pbT = nc.alloc_psum_tensor("pbT", [128, 1024], BF16)
        pbC = nc.alloc_psum_tensor("pbC", [128, 512], F32)
        pbD = nc.alloc_psum_tensor("pbD", [128, 512], F32)
        pbS = nc.alloc_psum_tensor("pbS", [128, 512], F32)
        pbK = nc.alloc_psum_tensor("pbK", [128, 8, 128], BF16)
        pbE = nc.alloc_psum_tensor("pbE", [128, 8, 128], BF16)

        op, dma = self.op, self.dma

        dma("sync", ident[:, :], id_in, "LCN")
        dma("sync", ident64[:, :], id64_in, "LCN")
        dma("sync", mub[:, :], mub_in, "LCN")
        dma("sync", muf[:, :], muf_in, "LCN")
        ev_gb = dma("sync", gbc[:, :], gnw_in, "LCN")
        ev_cb = dma("sync", cmbc[:, :], cm_in, "LCN")
        ev_svb = dma("sync", svbc[:, :], sv_in, "LCN")
        cn_all = ("LCN", self.cnt["LCN"])
        ev_fst = dma("sync", facA_d.ap()[:, :], fac_in, "LFC")

        op("vector", "memset", onesr[:, :], 1.0, inc="VE")
        ev_onec = op("vector", "memset", onec[:, :], 1.0, inc="VE")
        ev_ar = op("gpsimd", "collective_compute", "AllReduce", ALU.add,
                   replica_groups=[list(range(NCORES))],
                   ins=[facA_d.ap().opt()], outs=[facS_d.ap().opt()],
                   waits=[ev_fst], inc="CCA")
        ev_z1 = op("vector", "memset", HKr[:, :, :], 0.0, inc="VE")
        ev_z2 = op("vector", "memset", HVr[:, :, :], 0.0, inc="VE")
        ev_z3 = op("vector", "memset", Ar[:, :, :], 0.0, inc="VE")

        ev_xrd = [None, None]
        ev_hrd = [[], []]
        ev_hfm_evac = []
        tpose_evac = None
        for t in range(NCH):
            b = t % 2
            sem_x = "LX0" if b == 0 else "LX1"
            wl = [ev_xrd[b]] if ev_xrd[b] else []
            ev_x = dma("sync", xt[b][:, :], x_in[t * 128:(t + 1) * 128, :], sem_x,
                       waits=wl)
            c0 = t * 6
            ev_s1 = op("scalar", "activation", htm[b][:, :], xt[b][:, :], AF.Copy,
                       accum_out=stats[:, c0:c0 + 1],
                       waits=[ev_x] + ev_hrd[b], inc="SE")
            ev_v1 = op("vector", "tensor_scalar_mul", stats[:, c0 + 1:c0 + 2],
                       stats[:, c0:c0 + 1], -1.0 / D, waits=[ev_s1], inc="VE")
            ev_s2 = op("scalar", "activation", htm[b][:, :], xt[b][:, :], AF.Square,
                       bias=stats[:, c0 + 1:c0 + 2], accum_out=stats[:, c0 + 2:c0 + 3],
                       waits=[ev_v1], inc="SE")
            ev_v2 = op("vector", "tensor_scalar", stats[:, c0 + 3:c0 + 4],
                       stats[:, c0 + 2:c0 + 3], 1.0 / D, 1e-5, ALU.mult, ALU.add,
                       waits=[ev_s2], inc="VE")
            ev_s3 = op("scalar", "activation", stats[:, c0 + 4:c0 + 5],
                       stats[:, c0 + 3:c0 + 4], AF.Sqrt, waits=[ev_v2], inc="SE")
            ev_rcp = op("vector", "reciprocal", stats[:, c0 + 5:c0 + 6],
                        stats[:, c0 + 4:c0 + 5], waits=[ev_s3], inc="VE")
            ev_v4 = op("vector", "tensor_scalar", htm[b][:, :], xt[b][:, :],
                       stats[:, c0 + 1:c0 + 2], stats[:, c0 + 5:c0 + 6],
                       ALU.add, ALU.mult,
                       waits=[ev_s2, self.force(ev_v1), self.force(ev_rcp)],
                       inc="VE")
            ev_xrd[b] = ev_v4
            ev_sp = dma("sync", h_d.ap()[t * 128:(t + 1) * 128, :], htm[b][:, :], "LH",
                        waits=[ev_v4])
            last_te = None
            for r in range(2):
                for i in range(8):
                    dc = r * 8 + i
                    w = [ev_v4, cn_all]
                    if tpose_evac is not None:
                        w.append(tpose_evac)
                    last_te = op("tensor", "transpose",
                                 pbT[:, i * 128:(i + 1) * 128],
                                 htm[b][:, dc * 128:(dc + 1) * 128], ident[:, :],
                                 waits=w, inc="TE")
                tpose_evac = op("vector", "tensor_copy",
                                h_fm[:, r * 8:(r + 1) * 8, t * 128:(t + 1) * 128],
                                pbT[:, 0:1024].rearrange("p (a b) -> p a b", a=8),
                                waits=[last_te], inc="VE")
                ev_hfm_evac.append(tpose_evac)
            ev_hrd[b] = [last_te, ev_sp]
        h_ready = ev_hfm_evac[-1]
        lh_all = ("LH", self.cnt["LH"])
        if self.stop_after == "ph1":
            return self._finish_early(y_out, xt[0][:, :], [h_ready])
        self.tap("h_fm", h_fm[:, :, :], [h_ready], [128, NDC, 1024], BF16)

        plist = [("k", 0), ("k", 1), ("s", 0), ("s", 1),
                 ("v", 0), ("v", 1), ("v", 2), ("v", 3),
                 ("q", 0), ("q", 1)]
        wbufs = [wbufA, wbufB]
        wsems = ["LWA", "LWB"]
        ev_wfree = [None, None]
        bank_ev = {}
        ev_qe, ev_ke, ev_ee, ev_ve = [], [], [], []
        clip_rd = [None, None]
        def proj_piece(pi, ev_wl_pre=None):
            wn, pc = plist[pi]
            bi = pi % 2
            if ev_wl_pre is None:
                wl = [ev_wfree[bi]] if ev_wfree[bi] else []
                ev_wl = dma("gpsimd", wbufs[bi][:, :, :],
                            w3[wn][:, :, pc * 512:(pc + 1) * 512], wsems[bi],
                            waits=wl)
            else:
                ev_wl = ev_wl_pre
            last_mm = None
            if wn in ("q", "k", "s"):
                for g in range(4):
                    for tt in range(2):
                        pb, tag = (pbA, "A") if (g * 2 + tt) % 2 == 0 else (pbB, "B")
                        for dc in range(NDC):
                            w = [ev_wl, h_ready]
                            if dc == 0 and bank_ev.get(tag):
                                w.append(bank_ev[tag])
                            last_mm = op("tensor", "matmul", pb[:, 0:512],
                                         wbufs[bi][:, dc, g * 128:(g + 1) * 128],
                                         h_fm[:, dc, tt * 512:(tt + 1) * 512],
                                         start=(dc == 0), stop=(dc == NDC - 1),
                                         waits=w,
                                         inc="TE" if dc == NDC - 1 else None)
                        cg = pc * 4 + g
                        if wn == "q":
                            ev = op("vector", "tensor_scalar_mul",
                                    q_fm[:, cg, tt * 512:(tt + 1) * 512],
                                    pb[:, 0:512], 0.125, waits=[last_mm], inc="VE")
                            ev_qe.append(ev)
                            bank_ev[tag] = ev
                        elif wn == "k":
                            ev = op("vector", "tensor_copy",
                                    k_fm[:, cg, tt * 512:(tt + 1) * 512],
                                    pb[:, 0:512], waits=[last_mm], inc="VE")
                            ev_ke.append(ev)
                            bank_ev[tag] = ev
                        else:
                            cb = (g * 2 + tt) % 2
                            w2_ = [last_mm] + ([clip_rd[cb]] if clip_rd[cb] else [])
                            evc = op("vector", "tensor_scalar", clipsc[cb][:, :],
                                     pb[:, 0:512], 32.0, -32.0, ALU.min, ALU.max,
                                     waits=w2_, inc="VE")
                            bank_ev[tag] = evc
                            eve = op("scalar", "activation",
                                     e_fm[:, cg, tt * 512:(tt + 1) * 512],
                                     clipsc[cb][:, :], AF.Exp, waits=[evc], inc="SE")
                            ev_ee.append(eve)
                            clip_rd[cb] = eve
            else:
                for tt8 in range(NCH):
                    pb, tag = (pbC, "C") if tt8 % 2 == 0 else (pbD, "D")
                    for dc in range(NDC):
                        w = [ev_wl, h_ready]
                        if dc == 0 and bank_ev.get(tag):
                            w.append(bank_ev[tag])
                        last_mm = op("tensor", "matmul", pb[:, 0:512],
                                     h_fm[:, dc, tt8 * 128:(tt8 + 1) * 128],
                                     wbufs[bi][:, dc, 0:512],
                                     start=(dc == 0), stop=(dc == NDC - 1),
                                     waits=w, inc="TE" if dc == NDC - 1 else None)
                    ev = op("vector", "tensor_copy",
                            v_tm[:, tt8, pc * 512:(pc + 1) * 512],
                            pb[:, 0:512], waits=[last_mm], inc="VE")
                    ev_ve.append(ev)
                    bank_ev[tag] = ev
            ev_wfree[bi] = last_mm
        for pi in range(8):
            proj_piece(pi)
        q_wl = []
        for pi in (8, 9):
            wn, pc = plist[pi]
            bi = pi % 2
            q_wl.append(dma("gpsimd", wbufs[bi][:, :, :],
                            w3[wn][:, :, pc * 512:(pc + 1) * 512], wsems[bi],
                            waits=[ev_wfree[bi]]))
        k_ready = ev_ke[-1]
        e_ready = ev_ee[-1]
        v_ready = ev_ve[-1]
        self.tap("e_fm", e_fm[:, :, :], [e_ready], [128, 8, 1024], BF16)
        self.tap("v_tm", v_tm[:, :, :], [v_ready], [128, NCH, 2048], BF16)

        ev_ktm = None
        ev_etm = None
        st_add = {}
        bank_rd = {}
        st_sbufrd = None

        def chunk_transposes(c, extra_waits):
            nonlocal ev_ktm, ev_etm
            lt_k = None
            lt_e = None
            for h2 in range(8):
                w = list(extra_waits)
                if h2 == 0 and ev_ktm is not None:
                    w.append(ev_ktm)
                lt_k = op("tensor", "transpose", pbK[:, h2, :],
                          k_fm[:, h2, c * 128:(c + 1) * 128],
                          ident[:, :], waits=w,
                          inc="TE" if h2 == 7 else None)
            for h2 in range(8):
                w = []
                if h2 == 0 and ev_etm is not None:
                    w.append(ev_etm)
                lt_e = op("tensor", "transpose", pbE[:, h2, :],
                          e_fm[:, h2, c * 128:(c + 1) * 128],
                          ident[:, :], waits=w,
                          inc="TE" if h2 == 7 else None)
            w = [lt_k] + ([st_sbufrd] if st_sbufrd else [])
            ev_ktm = op("vector", "tensor_copy",
                        ktm[:, :, :].rearrange("p a b -> p (a b)"),
                        pbK[:, :, :].rearrange("p a b -> p (a b)"),
                        waits=w, inc="VE")
            ev_etm = op("vector", "tensor_copy",
                        etm[:, :, :].rearrange("p a b -> p (a b)"),
                        pbE[:, :, :].rearrange("p a b -> p (a b)"),
                        waits=[lt_e], inc="VE")

        def state_quad(c, Q, extra_waits):
            nonlocal st_sbufrd
            w = list(extra_waits) + [ev_ktm, ev_etm]
            if bank_rd.get("S"):
                w.append(bank_rd["S"])
            teS1 = None
            for i in range(4):
                h = 4 * Q + i
                bp = (i % 2) * 64
                col = (i % 2) * 128 + (i // 2) * 64
                teS1 = op("tensor", "matmul", pbS[bp:bp + 64, col:col + 64],
                          ktm[:, h, :], etm[:, h, :], start=True, stop=True,
                          waits=w if i == 0 else ())
            for i in range(2):
                h = 4 * Q + i
                bp = (i % 2) * 64
                teS1 = op("tensor", "matmul",
                          pbS[bp:bp + 64, 256 + i * 128:256 + (i + 1) * 128],
                          etm[:, h, :], v_tm[:, c, h * 128:(h + 1) * 128],
                          start=True, stop=True,
                          inc="TE" if i == 1 else None)
            wC = [ev_onec]
            if bank_rd.get("C"):
                wC.append(bank_rd["C"])
            op("tensor", "matmul", pbC[0:1, 0:256], onec[:, :],
               etm[:, 4 * Q:4 * Q + 4, :].rearrange("p a b -> p (a b)"),
               start=True, stop=True, waits=wC)
            teS2 = None
            for i in range(2, 4):
                h = 4 * Q + i
                bp = (i % 2) * 64
                teS2 = op("tensor", "matmul",
                          pbC[bp:bp + 64, 256 + (i - 2) * 128:256 + (i - 1) * 128],
                          etm[:, h, :], v_tm[:, c, h * 128:(h + 1) * 128],
                          start=True, stop=True,
                          inc="TE" if i == 3 else None)
            st_sbufrd = teS2
            a1 = op("vector", "tensor_tensor",
                    HKr[0:64, 2 * Q:2 * Q + 2, :].rearrange("p a b -> p (a b)"),
                    pbS[0:64, 0:128],
                    HKr[0:64, 2 * Q:2 * Q + 2, :].rearrange("p a b -> p (a b)"),
                    ALU.add, waits=[teS1, teS2], inc="VE")
            op("vector", "tensor_tensor",
               HKr[64:128, 2 * Q:2 * Q + 2, :].rearrange("p a b -> p (a b)"),
               pbS[64:128, 128:256],
               HKr[64:128, 2 * Q:2 * Q + 2, :].rearrange("p a b -> p (a b)"),
               ALU.add)
            op("vector", "tensor_tensor", HVr[0:64, 2 * Q, :],
               pbS[0:64, 256:384], HVr[0:64, 2 * Q, :], ALU.add)
            a4 = op("vector", "tensor_tensor", HVr[64:128, 2 * Q, :],
                    pbS[64:128, 384:512], HVr[64:128, 2 * Q, :], ALU.add, inc="VE")
            op("vector", "tensor_tensor", HVr[0:64, 2 * Q + 1, :],
               pbC[0:64, 256:384], HVr[0:64, 2 * Q + 1, :], ALU.add)
            op("vector", "tensor_tensor", HVr[64:128, 2 * Q + 1, :],
               pbC[64:128, 384:512], HVr[64:128, 2 * Q + 1, :], ALU.add)
            a7 = op("vector", "tensor_tensor", Ar[0:1, 4 * Q:4 * Q + 4, :],
                    pbC[0:1, 0:256].rearrange("p (a b) -> p a b", a=4),
                    Ar[0:1, 4 * Q:4 * Q + 4, :], ALU.add, inc="VE")
            bank_rd["S"] = a4
            bank_rd["C"] = a7
            st_add["bankS"] = a7
            return a7

        last_p1 = None
        nch3 = NCH
        nh3 = H
        if self.stop_after in ("ph3t", "ph3s1", "ph3s2", "ph3s16", "ph3c2", "ph3c4"):
            nch3 = {"ph3t": 1, "ph3s1": 1, "ph3s2": 1, "ph3s16": 1,
                    "ph3c2": 2, "ph3c4": 4}[self.stop_after]
            nh3 = {"ph3t": 0, "ph3s1": 1, "ph3s2": 2, "ph3s16": H,
                   "ph3c2": H, "ph3c4": H}[self.stop_after]
        for c in range(nch3):
            chunk_transposes(c, [k_ready, e_ready, ev_z1, ev_z2, ev_z3])
            for Q in range((nh3 + 3) // 4):
                last_p1 = state_quad(c, Q, [v_ready])
        if self.stop_after in ("ph3t", "ph3s1", "ph3s2", "ph3s16", "ph3c2", "ph3c4"):
            return self._finish_early(y_out, xt[0][:, :],
                                      [ev_ktm, ev_etm] + ([last_p1] if last_p1 else []))
        self.tap("hk_loc", HKr[:, :, :], [last_p1], [128, 8, 64], F32)
        if self.stop_after == "ph3":
            return self._finish_early(y_out, xt[0][:, :], [last_p1])

        proj_piece(8, q_wl[0])
        proj_piece(9, q_wl[1])
        q_ready = ev_qe[-1]
        self.tap("q_fm", q_fm[:, :, :], [q_ready], [128, 8, 1024], BF16)
        if self.stop_after == "ph2":
            return self._finish_early(y_out, xt[0][:, :],
                                      [ev_qe[-1], ev_ke[-1], ev_ee[-1], ev_ve[-1]])

        dma("sync", stI_d.ap()[0:65536].rearrange("(p n) -> p n", p=128),
            HKr[:, :, :].rearrange("p a b -> p (a b)"), "LST", waits=[last_p1])
        dma("sync", stI_d.ap()[65536:196608].rearrange("(p n) -> p n", p=128),
            HVr[:, :, :].rearrange("p a b -> p (a b)"), "LST")
        dma("sync", stI_d.ap()[196608:197632].rearrange("(p n) -> p n", p=1),
            Ar[:, :, :].rearrange("p a b -> p (a b)"), "LST")
        st_all = ("LST", self.cnt["LST"])
        ev_ag = op("gpsimd", "collective_compute", "AllGather", ALU.bypass,
                   replica_groups=[list(range(NCORES))],
                   ins=[stI_d.ap().opt()], outs=[stO_d.ap().opt()],
                   waits=[st_all], inc="CCG")
        ev_m1 = op("vector", "memset", HKr[:, :, :], 0.0, waits=[st_all], inc="VE")
        op("vector", "memset", HVr[:, :, :], 0.0, inc="VE")
        op("vector", "memset", Ar[:, :, :], 0.0, inc="VE")
        ev_msum = None
        for i in range(NCORES):
            base = i * STATE_N
            wl = [ev_ag] if i == 0 else [ev_msum]
            dma("sync", lndHK[:, :],
                stO_d.ap()[base:base + 65536].rearrange("(p n) -> p n", p=128),
                "LLD", waits=wl)
            dma("sync", lndHV[:, :],
                stO_d.ap()[base + 65536:base + 196608].rearrange("(p n) -> p n", p=128),
                "LLD")
            dma("sync", lndA[:, :],
                stO_d.ap()[base + 196608:base + 197632].rearrange("(p n) -> p n", p=1),
                "LLD")
            ld_all = ("LLD", self.cnt["LLD"])
            op("vector", "scalar_tensor_tensor", HKr[:, :, :],
               lndHK[:, :].rearrange("p (a b) -> p a b", a=8), cmbc[:, i:i + 1],
               HKr[:, :, :], ALU.mult, ALU.add,
               waits=[ld_all, ev_m1, ev_cb, self.force(ev_msum)], inc="VE")
            op("vector", "scalar_tensor_tensor", HVr[:, :, :],
               lndHV[:, :].rearrange("p (a b) -> p a b", a=8), cmbc[:, i:i + 1],
               HVr[:, :, :], ALU.mult, ALU.add, inc="VE")
            ev_msum = op("vector", "scalar_tensor_tensor", Ar[:, :, :],
                         lndA[:, :].rearrange("p (a b) -> p a b", a=16),
                         cmbc[0:1, i:i + 1], Ar[:, :, :], ALU.mult, ALU.add,
                         inc="VE")
            st_add["bankS"] = ev_msum
        self.tap("s_in", HKr[:, :, :], [ev_msum], [128, 8, 64], F32)
        if self.stop_after == "ph4":
            return self._finish_early(y_out, xt[0][:, :], [ev_msum])

        slot_ev = {}
        ev_oev = None
        ev_sq_t = {}
        for c in range(NCH):
            wst = [self.force(st_add.get("bankS")), self.force(ev_msum)]
            if self.taps_req:
                wst.append(("SO", self.cnt["SO"]))
            evK = op("vector", "tensor_copy", HKb[:, :, :], HKr[:, :, :],
                     waits=wst, inc="VE")
            evV = op("vector", "tensor_copy", HVb[:, :, :], HVr[:, :, :], inc="VE")
            evA = op("vector", "tensor_copy", Ab[:, :, :], Ar[:, :, :], inc="VE")
            chunk_transposes(c, [ev_msum])
            for h in range(H):
                bp = (h % 2) * 64
                sl = h % 2
                c0 = h * 4
                w = [ev_ktm, ev_etm, evK, evV, evA, q_ready]
                if slot_ev.get("rdA"):
                    w.append(slot_ev["rdA"])
                mmQ = op("tensor", "matmul", pbA[:, sl * 128:(sl + 1) * 128],
                         k_fm[bp:bp + 64, h // 2, c * 128:(c + 1) * 128],
                         q_fm[bp:bp + 64, h // 2, c * 128:(c + 1) * 128],
                         start=True, stop=True, waits=w, inc="TE")
                ev_mq = op("vector", "tensor_tensor", mq_s[:, sl, :],
                           pbA[:, sl * 128:(sl + 1) * 128], muf[:, :], ALU.mult,
                           waits=[mmQ], inc="VE")
                slot_ev["rdA"] = ev_mq
                w = [ev_mq]
                if slot_ev.get("rdB"):
                    w.append(slot_ev["rdB"])
                op("tensor", "matmul", pbB[:, sl * 128:sl * 128 + 64],
                   mq_s[:, sl, :], etm[:, h, :], start=True, stop=False, waits=w)
                op("tensor", "matmul", pbB[:, sl * 128:sl * 128 + 64],
                   q_fm[bp:bp + 64, h // 2, c * 128:(c + 1) * 128],
                   HKb[bp:bp + 64, h // 2, :], start=False, stop=True)
                op("tensor", "matmul", pbB[:, sl * 128 + 64:sl * 128 + 128],
                   mub[:, :], etm[:, h, :], start=True, stop=False)
                mmD = op("tensor", "matmul", pbB[:, sl * 128 + 64:sl * 128 + 128],
                         onesr[:, :], Ab[0:1, h, :], start=False, stop=True,
                         inc="TE")
                if c == NCH - 1 and h == H - 1:
                    te_qk_free = ("TE", self.cnt["TE"])
                ev_rc = op("vector", "reciprocal", rcp[:, h, :],
                           pbB[:, sl * 128 + 64:sl * 128 + 128], waits=[mmD],
                           inc="VE")
                ev_ok = op("vector", "tensor_tensor", okb[:, h, :],
                           pbB[:, sl * 128:sl * 128 + 64], rcp[:, h, :], ALU.mult,
                           waits=[self.force(ev_rc)], inc="VE")
                slot_ev["rdB"] = ev_ok
                ev_mx = op("vector", "reduce_max", rowst[:, c0:c0 + 1],
                           okb[:, h, :], axis=AXX, negate=True,
                           waits=[self.force(ev_ok)], inc="VE")
                ev_ex = op("scalar", "activation", pex[:, h, :], okb[:, h, :],
                           AF.Exp, bias=rowst[:, c0:c0 + 1], waits=[ev_mx],
                           inc="SE")
                ev_sm = op("vector", "reduce_sum", rowst[:, c0 + 1:c0 + 2],
                           pex[:, h, :], axis=AXX, waits=[ev_ex], inc="VE")
                ev_sr2 = op("vector", "reciprocal", rowst[:, c0 + 2:c0 + 3],
                            rowst[:, c0 + 1:c0 + 2],
                            waits=[self.force(ev_sm)], inc="VE")
                w = [slot_ev[("wtm", sl)]] if slot_ev.get(("wtm", sl)) else []
                ev_w = op("vector", "scalar_tensor_tensor", wtm_s[:, sl, :],
                          pex[:, h, :], rowst[:, c0 + 2:c0 + 3], rcp[:, h, :],
                          ALU.mult, ALU.mult,
                          waits=w + [self.force(ev_sr2), self.force(ev_rc)],
                          inc="VE")
                w = [ev_w]
                if slot_ev.get("rdT"):
                    w.append(slot_ev["rdT"])
                mmW = op("tensor", "transpose",
                         pbT[bp:bp + 64, sl * 128:(sl + 1) * 128],
                         wtm_s[:, sl, :], ident[:, :], waits=w, inc="TE")
                ev_wf = op("vector", "tensor_copy", wfm_s[bp:bp + 64, sl, :],
                           pbT[bp:bp + 64, sl * 128:(sl + 1) * 128],
                           waits=[mmW], inc="VE")
                slot_ev[("wtm", sl)] = mmW
                slot_ev["rdT"] = ev_wf
                w = [ev_wf]
                if slot_ev.get("rdC"):
                    w.append(slot_ev["rdC"])
                mmP = op("tensor", "matmul", pbC[:, sl * 128:(sl + 1) * 128],
                         e_fm[bp:bp + 64, h // 2, c * 128:(c + 1) * 128],
                         wfm_s[bp:bp + 64, sl, :], start=True, stop=True,
                         waits=w, inc="TE")
                ev_mp = op("vector", "tensor_tensor", mpet_s[:, sl, :],
                           pbC[:, sl * 128:(sl + 1) * 128], muf[:, :], ALU.mult,
                           waits=[mmP], inc="VE")
                slot_ev["rdC"] = ev_mp
                w = [ev_mp]
                if slot_ev.get("rdD"):
                    w.append(slot_ev["rdD"])
                op("tensor", "matmul", pbD[:, sl * 128:(sl + 1) * 128],
                   mpet_s[:, sl, :], v_tm[:, c, h * 128:(h + 1) * 128],
                   start=True, stop=False, waits=w)
                mmO = op("tensor", "matmul", pbD[:, sl * 128:(sl + 1) * 128],
                         wfm_s[bp:bp + 64, sl, :], HVb[bp:bp + 64, h // 2, :],
                         start=False, stop=True, inc="TE")
                ev_oev = op("vector", "tensor_copy",
                            o_tm[:, c, h * 128:(h + 1) * 128],
                            pbD[:, sl * 128:(sl + 1) * 128], waits=[mmO], inc="VE")
                slot_ev["rdD"] = ev_oev
                if h % 4 == 3:
                    a7q = state_quad(c, h // 4,
                                     [slot_ev.get("rdC")])
                    slot_ev["rdC"] = a7q
            ev_sq = None
            for h2 in range(H):
                ev_sq = op("scalar", "activation", junkt[:, 0:128],
                           o_tm[:, c, h2 * 128:(h2 + 1) * 128], AF.Square,
                           accum_out=stats[:, 64 + c * 16 + h2:65 + c * 16 + h2],
                           waits=[ev_oev], inc="SE")
            ev_sq_t[c] = ev_sq
            if c == 0:
                self.tap("okb0", okb[:, :, :], [ev_oev], [128, 16, 64], F32)
                self.tap("pex0", pex[:, :, :], [ev_oev], [128, 16, 64], F32)
                self.tap("rcp0", rcp[:, :, :], [ev_oev], [128, 16, 64], F32)
                self.tap("rowst0", rowst[:, :], [ev_oev], [128, 64], F32)
                self.tap("hkb0", HKb[:, :, :], [ev_oev], [128, 8, 64], BF16)
                self.tap("hvb0", HVb[:, :, :], [ev_oev], [128, 8, 128], BF16)
                self.tap("abf0", Ab[:, :, :], [ev_oev], [1, 16, 64], BF16)
        o_done = ev_oev
        self.tap("okb", okb[:, :, :], [o_done], [128, 16, 64], F32)
        self.tap("pex", pex[:, :, :], [o_done], [128, 16, 64], F32)
        self.tap("rcp", rcp[:, :, :], [o_done], [128, 16, 64], F32)
        self.tap("hkb", HKb[:, :, :], [o_done], [128, 8, 64], BF16)
        self.tap("hvb", HVb[:, :, :], [o_done], [128, 8, 128], BF16)
        self.tap("abf", Ab[:, :, :], [o_done], [1, 16, 64], BF16)
        self.tap("rowst", rowst[:, :], [o_done], [128, 64], F32)
        self.tap("hkr", HKr[:, :, :], [o_done], [128, 8, 64], F32)
        self.tap("hvr", HVr[:, :, :], [o_done], [128, 8, 128], F32)
        tap_gate = ("SO", self.cnt["SO"]) if self.taps_req else None
        te_p2 = ("TE", self.cnt["TE"])
        ve_p2 = ("VE", self.cnt["VE"])
        se_p2 = ("SE", self.cnt["SE"])
        tpose_evac = ve_p2
        self.tap("o_tm", o_tm[:, :, :], [o_done], [128, NCH, 2048], BF16)
        if self.stop_after == "ph5":
            return self._finish_early(y_out, xt[0][:, :], [o_done])

        for p in range(4):
            wts = [ev_wfree[p % 2]] if p < 2 else [te_qk_free]
            dma("gpsimd", wgp[p][:, :, :],
                w3["g"][:, :, p * 512:(p + 1) * 512], "LWG", waits=wts)
        wg_all = ("LWG", self.cnt["LWG"])
        ev_rms = {}
        for t in range(NCH):
            ev_sq = ev_sq_t[t]
            r0 = 64 + t * 16
            i0 = 192 + t * 16
            ev_va = op("vector", "tensor_scalar", stats[:, i0:i0 + 16],
                       stats[:, r0:r0 + 16], 1.0 / DV, 1e-5, ALU.mult, ALU.add,
                       waits=[ev_sq], inc="VE")
            ev_sr = op("scalar", "activation", stats[:, i0:i0 + 16],
                       stats[:, i0:i0 + 16], AF.Sqrt, waits=[ev_va], inc="SE")
            ev_rms[t] = op("vector", "reciprocal", stats[:, i0:i0 + 16],
                           stats[:, i0:i0 + 16], waits=[ev_sr], inc="VE")
        ev_gt = {}
        ev_ht_rd = None
        ev_sg_rd = None
        bank_ev["gA"] = bank_ev["gB"] = ve_p2
        for t in range(NCH):
            ev_hl = dma("sync", htile[:, :], h_d.ap()[t * 128:(t + 1) * 128, :],
                        "LR0", waits=[lh_all, ve_p2, se_p2, ev_ht_rd, tap_gate])
            lt = None
            for r in range(2):
                for i in range(8):
                    dc = r * 8 + i
                    lt = op("tensor", "transpose", pbT[:, i * 128:(i + 1) * 128],
                            htile[:, dc * 128:(dc + 1) * 128], ident[:, :],
                            waits=[ev_hl, tpose_evac], inc="TE")
                tpose_evac = op("vector", "tensor_copy",
                                htfm[:, r * 8:(r + 1) * 8, :],
                                pbT[:, 0:1024].rearrange("p (a b) -> p a b", a=8),
                                waits=[lt], inc="VE")
            ev_ht_rd = lt
            last_g = None
            for p in range(4):
                pb, tag = (pbA, "gA") if p % 2 == 0 else (pbB, "gB")
                for dc in range(NDC):
                    w = [wg_all, tpose_evac]
                    if dc == 0:
                        w.append(bank_ev[tag])
                    last_g = op("tensor", "matmul", pb[:, 0:512],
                                htfm[:, dc, :],
                                wgp[p][:, dc, :],
                                start=(dc == 0), stop=(dc == NDC - 1),
                                waits=w, inc="TE" if dc == NDC - 1 else None)
                ev_sg = op("scalar", "activation", sgbuf[:, 0:512], pb[:, 0:512],
                           AF.Silu, waits=[last_g, ev_sg_rd], inc="SE")
                bank_ev[tag] = ev_sg
                evn = None
                for si in range(4):
                    hh = p * 4 + si
                    evn = op("vector", "scalar_tensor_tensor",
                             o_tm[:, t, hh * 128:(hh + 1) * 128],
                             o_tm[:, t, hh * 128:(hh + 1) * 128],
                             stats[:, 192 + t * 16 + hh:193 + t * 16 + hh],
                             gbc[:, :], ALU.mult, ALU.mult,
                             waits=[self.force(ev_rms[t]), ev_gb], inc="VE")
                ev_sg_rd = op("vector", "tensor_tensor",
                              o_tm[:, t, p * 512:(p + 1) * 512],
                              o_tm[:, t, p * 512:(p + 1) * 512],
                              sgbuf[:, 0:512], ALU.mult, waits=[ev_sg, evn],
                              inc="VE")
            ev_gt[t] = ev_sg_rd
        te_pA = ("TE", self.cnt["TE"])
        se_pA = ("SE", self.cnt["SE"])
        ve_pA = ("VE", self.cnt["VE"])
        self.tap("o_gated", o_tm[:, :, :], [ev_gt[NCH - 1]], [128, NCH, 2048], BF16)
        if self.stop_after == "ph6a":
            return self._finish_early(y_out, xt[0][:, :], [ev_gt[NCH - 1]])

        for p in range(4):
            dma("gpsimd", wop[p][:, :, :],
                w3["o"][:, :, p * 512:(p + 1) * 512], "LWO",
                waits=[te_p2, ve_p2])
        wo_all = ("LWO", self.cnt["LWO"])
        ev_h2ev = []
        ev_h2_rd = None
        ev_att_rd = None
        ev_xt2_rd = None
        ev_x1t_wr_gate = []
        last_x1sp = None
        bank_ev["oC"] = bank_ev["oD"] = ve_p2
        pending_h2 = None

        def emit_h2_tposes(pend):
            nonlocal tpose_evac, ev_h2_rd
            ev_nb_p, tp = pend
            lt2 = None
            for r in range(2):
                for i in range(8):
                    dc = r * 8 + i
                    lt2 = op("tensor", "transpose", pbT[:, i * 128:(i + 1) * 128],
                             h2tm[:, dc * 128:(dc + 1) * 128], ident[:, :],
                             waits=[ev_nb_p, tpose_evac], inc="TE")
                tpose_evac = op("vector", "tensor_copy",
                                h2_fm[:, r * 8:(r + 1) * 8, tp * 128:(tp + 1) * 128],
                                pbT[:, 0:1024].rearrange("p (a b) -> p a b", a=8),
                                waits=[lt2], inc="VE")
                ev_h2ev.append(tpose_evac)
            ev_h2_rd = lt2

        for t in range(NCH):
            lt = None
            for r in range(2):
                for i in range(8):
                    dvc = r * 8 + i
                    lt = op("tensor", "transpose", pbT[:, i * 128:(i + 1) * 128],
                            o_tm[:, t, dvc * 128:(dvc + 1) * 128], ident[:, :],
                            waits=[ev_gt[t], tpose_evac], inc="TE")
                tpose_evac = op("vector", "tensor_copy",
                                ofmt[:, r * 8:(r + 1) * 8, :],
                                pbT[:, 0:1024].rearrange("p (a b) -> p a b", a=8),
                                waits=[lt, se_p2], inc="VE")
            if t == NCH - 1:
                te_otm_free = ("TE", self.cnt["TE"])
            b = t % 2
            sem_x = "LX0" if b == 0 else "LX1"
            ev_x2 = dma("sync", xt2[:, :], x_in[t * 128:(t + 1) * 128, :], sem_x,
                        waits=[te_pA, se_pA, ve_pA, ev_xt2_rd])
            last_wo = None
            evat = None
            for ep in range(4):
                pb, tag = (pbC, "oC") if ep % 2 == 0 else (pbD, "oD")
                for dvc in range(NDC):
                    w = [wo_all, tpose_evac]
                    if dvc == 0:
                        w.append(bank_ev[tag])
                    last_wo = op("tensor", "matmul", pb[:, 0:512],
                                 ofmt[:, dvc, :],
                                 wop[ep][:, dvc, :],
                                 start=(dvc == 0), stop=(dvc == NDC - 1),
                                 waits=w, inc="TE" if dvc == NDC - 1 else None)
                evat = op("vector", "tensor_copy", attb[:, ep * 512:(ep + 1) * 512],
                          pb[:, 0:512], waits=[last_wo, se_p2, ev_att_rd], inc="VE")
                bank_ev[tag] = evat
            if pending_h2 is not None:
                emit_h2_tposes(pending_h2)
                pending_h2 = None
            c0 = 320 + t * 6
            ev_s1 = op("scalar", "activation", h2tm[:, :], attb[:, :], AF.Copy,
                       accum_out=stats[:, c0:c0 + 1], waits=[evat, ev_h2_rd],
                       inc="SE")
            ev_v1 = op("vector", "tensor_scalar_mul", stats[:, c0 + 1:c0 + 2],
                       stats[:, c0:c0 + 1], -1.0 / D, waits=[ev_s1], inc="VE")
            ev_s2 = op("scalar", "activation", h2tm[:, :], attb[:, :], AF.Square,
                       bias=stats[:, c0 + 1:c0 + 2], accum_out=stats[:, c0 + 2:c0 + 3],
                       waits=[ev_v1], inc="SE")
            ev_v2 = op("vector", "tensor_scalar", stats[:, c0 + 3:c0 + 4],
                       stats[:, c0 + 2:c0 + 3], 1.0 / D, 1e-5, ALU.mult, ALU.add,
                       waits=[ev_s2], inc="VE")
            ev_s3 = op("scalar", "activation", stats[:, c0 + 4:c0 + 5],
                       stats[:, c0 + 3:c0 + 4], AF.Sqrt, waits=[ev_v2], inc="SE")
            ev_rcpa = op("vector", "reciprocal", stats[:, c0 + 5:c0 + 6],
                         stats[:, c0 + 4:c0 + 5], waits=[ev_s3], inc="VE")
            ev_n = op("vector", "tensor_scalar", x1t[:, :], attb[:, :],
                      stats[:, c0 + 1:c0 + 2], stats[:, c0 + 5:c0 + 6],
                      ALU.add, ALU.mult,
                      waits=[ev_s2, self.force(ev_v1), self.force(ev_rcpa)]
                      + ev_x1t_wr_gate, inc="VE")
            ev_att_rd = ev_n
            ev_r = op("vector", "tensor_tensor", x1t[:, :], x1t[:, :], xt2[:, :],
                      ALU.add, waits=[ev_x2, self.force(ev_n)], inc="VE")
            ev_xt2_rd = ev_r
            esp = dma("sync", x1_d.ap()[t * 128:(t + 1) * 128, :], x1t[:, :], "LR1",
                      waits=[ev_r])
            last_x1sp = esp
            c1 = 368 + t * 6
            ev_s1b = op("scalar", "activation", h2tm[:, :], x1t[:, :], AF.Copy,
                        accum_out=stats[:, c1:c1 + 1], waits=[ev_r], inc="SE")
            ev_v1b = op("vector", "tensor_scalar_mul", stats[:, c1 + 1:c1 + 2],
                        stats[:, c1:c1 + 1], -1.0 / D, waits=[ev_s1b], inc="VE")
            ev_s2b = op("scalar", "activation", h2tm[:, :], x1t[:, :], AF.Square,
                        bias=stats[:, c1 + 1:c1 + 2],
                        accum_out=stats[:, c1 + 2:c1 + 3], waits=[ev_v1b], inc="SE")
            ev_v2b = op("vector", "tensor_scalar", stats[:, c1 + 3:c1 + 4],
                        stats[:, c1 + 2:c1 + 3], 1.0 / D, 1e-5, ALU.mult, ALU.add,
                        waits=[ev_s2b], inc="VE")
            ev_s3b = op("scalar", "activation", stats[:, c1 + 4:c1 + 5],
                        stats[:, c1 + 3:c1 + 4], AF.Sqrt, waits=[ev_v2b], inc="SE")
            ev_rcpb = op("vector", "reciprocal", stats[:, c1 + 5:c1 + 6],
                         stats[:, c1 + 4:c1 + 5], waits=[ev_s3b], inc="VE")
            ev_nb = op("vector", "tensor_scalar", h2tm[:, :], x1t[:, :],
                       stats[:, c1 + 1:c1 + 2], stats[:, c1 + 5:c1 + 6],
                       ALU.add, ALU.mult,
                       waits=[ev_s2b, self.force(ev_v1b), self.force(ev_rcpb)],
                       inc="VE")
            ev_x1t_wr_gate = [esp, ev_s2b]
            pending_h2 = (ev_nb, t)
        emit_h2_tposes(pending_h2)
        h2_ready = ev_h2ev[-1]
        te_pB = ("TE", self.cnt["TE"])
        ve_pB = ("VE", self.cnt["VE"])
        se_pB = ("SE", self.cnt["SE"])
        lr1_all = ("LR1", self.cnt["LR1"])
        self.tap("h2_fm", h2_fm[:, :, :], [h2_ready], [128, NDC, 1024], BF16)
        if self.stop_after == "ph6b":
            return self._finish_early(y_out, x1t[:, :], [h2_ready, last_x1sp])

        for p in range(4):
            wts = [ev_ar, te_pA, se_pA] if p < 2 else [ev_ar, te_otm_free]
            dma("gpsimd", wsp[p][:, :, :],
                facS3[:, :, p * 512:(p + 1) * 512], "LWS", waits=wts)
        ws_all = ("LWS", self.cnt["LWS"])
        ev_bb = dma("sync", biasbc[:, :], bias_in, "LCN",
                    waits=[te_pB, ve_pB, se_pB])
        mix_rd = [None, None, None]
        bank_ev["fA"] = bank_ev["fB"] = se_pA
        first_gate = [se_pB, ve_pB, te_pB, lr1_all]
        ev_xrs = {0: dma("sync", x1r[:, :], x1_d.ap()[0:128, :], "LR0",
                         waits=[lr1_all, se_pB, ve_pB])}

        def emit_ffn_ln(p, evmx_p):
            mx = mixbs[p % 3]
            c0 = 416 + p * 6
            ev_s1 = op("scalar", "activation", junkt[:, :], mx[:, :], AF.Copy,
                       accum_out=stats[:, c0:c0 + 1], waits=[evmx_p], inc="SE")
            ev_v1 = op("vector", "tensor_scalar_mul", stats[:, c0 + 1:c0 + 2],
                       stats[:, c0:c0 + 1], -1.0 / D, waits=[ev_s1], inc="VE")
            ev_s2 = op("scalar", "activation", junkt[:, :], mx[:, :], AF.Square,
                       bias=stats[:, c0 + 1:c0 + 2], accum_out=stats[:, c0 + 2:c0 + 3],
                       waits=[ev_v1], inc="SE")
            ev_v2 = op("vector", "tensor_scalar", stats[:, c0 + 3:c0 + 4],
                       stats[:, c0 + 2:c0 + 3], 1.0 / D, 1e-5, ALU.mult, ALU.add,
                       waits=[ev_s2], inc="VE")
            ev_s3 = op("scalar", "activation", stats[:, c0 + 4:c0 + 5],
                       stats[:, c0 + 3:c0 + 4], AF.Sqrt, waits=[ev_v2], inc="SE")
            ev_rcpf = op("vector", "reciprocal", stats[:, c0 + 5:c0 + 6],
                         stats[:, c0 + 4:c0 + 5], waits=[ev_s3], inc="VE")
            ev_nf = op("vector", "tensor_scalar", mx[:, :], mx[:, :],
                       stats[:, c0 + 1:c0 + 2], stats[:, c0 + 5:c0 + 6],
                       ALU.add, ALU.mult,
                       waits=[ev_s2, self.force(ev_v1), self.force(ev_rcpf)],
                       inc="VE")
            ev_y = op("vector", "tensor_tensor", mx[:, :], mx[:, :], x1r[:, :],
                      ALU.add, waits=[ev_xrs[p], self.force(ev_nf)], inc="VE")
            mix_rd[p % 3] = dma("sync", y_out[p * 128:(p + 1) * 128, :], mx[:, :],
                                "SO", waits=[ev_y])
            if p + 1 < NCH:
                ev_xrs[p + 1] = dma("sync", x1r[:, :],
                                    x1_d.ap()[(p + 1) * 128:(p + 2) * 128, :],
                                    "LR0" if p % 2 == 0 else "LR1",
                                    waits=[ev_y])

        pend_ln = None
        for t in range(NCH):
            b = t % 3
            mx = mixbs[b]
            last_f = None
            evmx = None
            for ep in range(4):
                pb, tag = (pbA, "fA") if ep % 2 == 0 else (pbB, "fB")
                for dc in range(NDC):
                    w = [ws_all, h2_ready]
                    if dc == 0:
                        w.append(bank_ev[tag])
                    last_f = op("tensor", "matmul", pb[:, 0:512],
                                h2_fm[:, dc, t * 128:(t + 1) * 128],
                                wsp[ep][:, dc, :],
                                start=(dc == 0), stop=(dc == NDC - 1),
                                waits=w, inc="TE" if dc == NDC - 1 else None)
                evmx = op("vector", "scalar_tensor_tensor",
                          mx[:, ep * 512:(ep + 1) * 512], pb[:, 0:512],
                          svbc[:, 0:1], biasbc[:, ep * 512:(ep + 1) * 512],
                          ALU.mult, ALU.add,
                          waits=[last_f, ev_svb, ev_bb, mix_rd[b]] + first_gate,
                          inc="VE")
                first_gate = []
                bank_ev[tag] = evmx
            if pend_ln is not None:
                emit_ffn_ln(*pend_ln)
            pend_ln = (t, evmx)
        emit_ffn_ln(*pend_ln)

        semh = self.semh
        so_final = self.cnt["SO"]
        self.prog["sync"].append(lambda e: e.wait_ge(semh["SO"], so_final))
        return self._emit()

    def _emit(self):
        nc = self.nc
        from contextlib import ExitStack
        with ExitStack() as st:
            for s in self.cnt:
                self.semh[s] = st.enter_context(nc.semaphore(s))
            with nc.Block() as blk:
                for eng in ("sync", "gpsimd", "tensor", "vector", "scalar"):
                    lst = self.prog[eng]

                    def body(e, lst=lst):
                        for f in lst:
                            f(e)
                    getattr(blk, eng)(body)
        return nc


def build_nc(taps=(), stop_after=None):
    return Builder(taps=taps, stop_after=stop_after).build()


def kernel(L0_attn_hidden_states, L0_attn_Wq, L0_attn_Wk, L0_attn_Wv, L0_attn_Wg,
           L0_attn_Ws, L0_attn_Wo, L0_attn_g_norm_weight, L0_ffn_factors,
           L0_ffn_bias):
    x = np.ascontiguousarray(np.asarray(L0_attn_hidden_states, dtype=np.float32)
                             .reshape(NCORES * TC, D))
    wq = np.ascontiguousarray(np.asarray(L0_attn_Wq, dtype=np.float32))
    wk = np.ascontiguousarray(np.asarray(L0_attn_Wk, dtype=np.float32))
    ws = np.ascontiguousarray(np.asarray(L0_attn_Ws, dtype=np.float32))
    wv = np.ascontiguousarray(np.asarray(L0_attn_Wv, dtype=np.float32))
    wg = np.ascontiguousarray(np.asarray(L0_attn_Wg, dtype=np.float32))
    wo = np.ascontiguousarray(np.asarray(L0_attn_Wo, dtype=np.float32))
    gnw = np.ascontiguousarray(np.tile(
        np.asarray(L0_attn_g_norm_weight, dtype=np.float32).reshape(1, DV), (128, 1)))
    fac = np.ascontiguousarray(np.asarray(L0_ffn_factors, dtype=np.float32))
    fb = np.ascontiguousarray(np.tile(
        np.asarray(L0_ffn_bias, dtype=np.float32).reshape(1, D), (128, 1)))
    sv = np.zeros((128, 8), np.float32)
    sv[:, 0] = 1.0 / math.sqrt(8.0)
    ident = np.eye(128).astype(ml_dtypes.bfloat16)
    ident64 = np.vstack([np.eye(64), np.eye(64)]).astype(ml_dtypes.bfloat16)
    masku_f = np.triu(np.ones((128, 128), np.float32))
    masku_b = masku_f.astype(ml_dtypes.bfloat16)

    nc = build_nc()
    in_maps = []
    for j in range(NCORES):
        cm = np.zeros((128, 8), np.float32)
        cm[:, :j] = 1.0
        in_maps.append({
            "x": x[j * TC:(j + 1) * TC],
            "wq": wq, "wk": wk, "wsl": ws, "wv": wv, "wg": wg, "wo": wo,
            "factor": fac[j],
            "gnw": gnw, "fbias": fb, "svec": sv, "cmask": cm,
            "ident": ident, "ident64": ident64, "masku_bf": masku_b, "masku_f": masku_f,
        })
    res = run_bass_kernel_spmd(nc, in_maps, core_ids=list(range(NCORES)))
    y = np.concatenate([r["y"] for r in res.results], axis=0)
    return y.reshape(1, NCORES * TC, D)
